# version 16
# baseline (speedup 1.0000x reference)
import numpy as np
import concourse.bass as bass
import concourse.mybir as mybir
from concourse.ap import AP
from concourse.bass_utils import run_bass_kernel_spmd
import ml_dtypes
from contextlib import ExitStack

F32 = mybir.dt.float32
BF16 = mybir.dt.bfloat16
AX = mybir.AxisListType.X
ALU = mybir.AluOpType
ACT = mybir.ActivationFunctionType

D = 1024
T = 2048
H = 16
NC = 8
TG = 4
QT = 16
KT = 16
FF = 12288
FC = 96
FBLK = 8
NBLK = FC // FBLK
ITERK = 4096
RNOV = 0.7
EPS_LN = 1e-5
SW = 2432
SLOT = 128 * (SW + 1) + 2560
NIT = 16
NPROBE = 7

ARENA_COLS = 94208
R0 = 0
R1 = 16384
R2 = 32768
R3 = 49152
R4 = 65536
R5 = 81920
SMALL_COLS = 2560


def build_nc(stage="full"):
    nc = bass.Bass("TRN2", target_bir_lowering=False)

    xT_d = nc.dram_tensor("xT", [D, T], F32, kind="ExternalInput")
    wq_d = nc.dram_tensor("wq", [D, D], F32, kind="ExternalInput")
    wk_d = nc.dram_tensor("wk", [D, D], F32, kind="ExternalInput")
    wv_d = nc.dram_tensor("wv", [D, D], F32, kind="ExternalInput")
    wo_d = nc.dram_tensor("wo", [D, D], F32, kind="ExternalInput")
    bq_d = nc.dram_tensor("bq", [128, 8], F32, kind="ExternalInput")
    bk_d = nc.dram_tensor("bk", [128, 8], F32, kind="ExternalInput")
    bv_d = nc.dram_tensor("bv", [128, 8], F32, kind="ExternalInput")
    bo_d = nc.dram_tensor("bo", [128, 8], F32, kind="ExternalInput")
    rextT_d = nc.dram_tensor("rextT", [64, 4096], F32, kind="ExternalInput")
    wmT_d = nc.dram_tensor("wmT", [D, FF], F32, kind="ExternalInput")
    wpT_d = nc.dram_tensor("wpT", [FF, D], F32, kind="ExternalInput")
    wc1T_d = nc.dram_tensor("wc1T", [D, 256], F32, kind="ExternalInput")
    wc2T_d = nc.dram_tensor("wc2T", [256, FF], F32, kind="ExternalInput")
    onesf_d = nc.dram_tensor("onesf", [128, 128], F32, kind="ExternalInput")
    identb_d = nc.dram_tensor("identb", [128, 128], BF16, kind="ExternalInput")
    out_d = nc.dram_tensor("outT", [D, T], F32, kind="ExternalOutput")

    skew_d = nc.dram_tensor("skew", [2 * QT * SLOT], BF16, kind="Internal")
    wq_bf = nc.dram_tensor("wq_bf", [D, D], BF16, kind="Internal")
    wk_bf = nc.dram_tensor("wk_bf", [D, D], BF16, kind="Internal")
    wv_bf = nc.dram_tensor("wv_bf", [D, D], BF16, kind="Internal")
    wo_bf = nc.dram_tensor("wo_bf", [D, D], BF16, kind="Internal")
    wm_bf = nc.dram_tensor("wm_bf", [D, FF], BF16, kind="Internal")
    wp_bf = nc.dram_tensor("wp_bf", [FF, D], BF16, kind="Internal")
    wc2_bf = nc.dram_tensor("wc2_bf", [256, FF], BF16, kind="Internal")
    y_spill = nc.dram_tensor("y_spill", [D, T], F32, kind="Internal")
    x1_spill = nc.dram_tensor("x1_spill", [D, T], F32, kind="Internal")
    xn1_spill = nc.dram_tensor("xn1_spill", [D, T], F32, kind="Internal")
    xo2_spill = nc.dram_tensor("xo2_spill", [D, T], F32, kind="Internal")
    z_spill = nc.dram_tensor("z_spill", [D, T], F32, kind="Internal")
    bounce_d = nc.dram_tensor("bounce", [1, 8], F32, kind="Internal")
    nov_in = [nc.dram_tensor(f"nov_in{i}", [1, 1], F32, kind="Internal")
              for i in range(2)]
    nov_out = [nc.dram_tensor(f"nov_out{i}", [1, 1], F32, kind="Internal",
                              addr_space="Shared") for i in range(2)]

    dbg = {}
    if stage in ("qkv", "attn", "x1", "ffn"):
        dbg["d_qt"] = nc.dram_tensor("d_qt", [D, T], BF16, kind="ExternalOutput")
        dbg["d_kt"] = nc.dram_tensor("d_kt", [D, T], BF16, kind="ExternalOutput")
        dbg["d_vt"] = nc.dram_tensor("d_vt", [D, T], BF16, kind="ExternalOutput")
    if stage in ("attn", "x1", "ffn"):
        dbg["d_ctx"] = nc.dram_tensor("d_ctx", [D, T], BF16, kind="ExternalOutput")
    if stage == "attn":
        dbg["d_sc0"] = nc.dram_tensor("d_sc0", [128, 2048], BF16, kind="ExternalOutput")
        dbg["d_band0"] = nc.dram_tensor("d_band0", [128, 2048], BF16, kind="ExternalOutput")
        dbg["d_attn0"] = nc.dram_tensor("d_attn0", [128, 2048], BF16, kind="ExternalOutput")
        dbg["d_bq0"] = nc.dram_tensor("d_bq0", [128, 2176], BF16, kind="ExternalOutput")
        dbg["d_negc"] = nc.dram_tensor("d_negc", [128, 16], F32, kind="ExternalOutput")
        dbg["d_srow"] = nc.dram_tensor("d_srow", [128, 1], F32, kind="ExternalOutput")
        dbg["d_sm"] = nc.dram_tensor("d_sm", [2, 64], F32, kind="ExternalOutput")
    if stage in ("x1", "ffn"):
        dbg["d_x1"] = nc.dram_tensor("d_x1", [D, T], F32, kind="ExternalOutput")
    if stage == "x1":
        dbg["d_y"] = nc.dram_tensor("d_y", [D, T], F32, kind="ExternalOutput")
        dbg["d_stat"] = nc.dram_tensor("d_stat", [2, 1, 2048], F32, kind="ExternalOutput")
    if stage == "ffn":
        dbg["d_xn"] = nc.dram_tensor("d_xn", [3, D, T], F32, kind="ExternalOutput")
        dbg["d_msk"] = nc.dram_tensor("d_msk", [4, 128, 96], F32,
                                      kind="ExternalOutput")
        dbg["d_sc"] = nc.dram_tensor("d_sc", [2, 128, 96], F32,
                                     kind="ExternalOutput")

    es = ExitStack()
    A = es.enter_context(nc.sbuf_tensor("arena", [128, ARENA_COLS], BF16))
    S = es.enter_context(nc.sbuf_tensor("smalls", [128, SMALL_COLS], F32))
    rextT_s = es.enter_context(nc.sbuf_tensor("rextT_s", [128, 4096], BF16))
    onesf_s = es.enter_context(nc.sbuf_tensor("onesf_s", [128, 128], F32))
    identb_s = es.enter_context(nc.sbuf_tensor("identb_s", [128, 128], BF16))
    ps0 = es.enter_context(nc.psum_tensor("ps0", [128, 512], F32))
    ps1 = es.enter_context(nc.psum_tensor("ps1", [128, 512], F32))
    pstrA = es.enter_context(nc.psum_tensor("pstrA", [128, 1024], BF16))
    pstrB = es.enter_context(nc.psum_tensor("pstrB", [128, 1024], BF16))
    ps4 = es.enter_context(nc.psum_tensor("ps4", [128, 512], F32))
    ps5 = es.enter_context(nc.psum_tensor("ps5", [128, 512], F32))
    ps6 = es.enter_context(nc.psum_tensor("ps6", [128, 512], F32))
    ps7 = es.enter_context(nc.psum_tensor("ps7", [128, 512], F32))
    sem = {}
    for name in ["dg", "ds", "t", "v", "s", "g", "cc", "tb", "vb", "ccb",
                 "gds", "gdg"]:
        sem[name] = es.enter_context(nc.semaphore("sem_" + name))
    blk = es.enter_context(nc.Block())

    q = {"tensor": [], "vector": [], "scalar": [], "sync": [], "gpsimd": []}
    cnt = {k: 0 for k in sem}
    ENG_SEM = {"tensor": "t", "vector": "v", "scalar": "s", "gpsimd": "g"}

    def emitq(e, f):
        q[e].append(f)

    QKEY = {"sync": "ds", "gpsimd": "dg"}

    def wait(e, ms):
        if ms is None:
            return
        key, val = ms
        if key in ("ds", "dg"):
            assert QKEY.get(e) == key, (e, ms)
            val = cnt[key]
        emitq(e, lambda eng, key=key, val=val: eng.wait_ge(sem[key], val))

    def op(e, fn, inc=False, sem_key=None):
        key = sem_key or ENG_SEM[e]
        if inc:
            cnt[key] += 1
            tgt = cnt[key]
            emitq(e, lambda eng, fn=fn, key=key: fn(eng).then_inc(sem[key], 1))
            return (key, tgt)
        emitq(e, fn)
        return None

    def dma(queue, out, in_, waits=(), sem_key=None):
        key = sem_key or QKEY[queue]
        for w in waits:
            if w is None:
                continue
            wk, wv = w
            if wk in ("ds", "dg"):
                if wk == QKEY[queue]:
                    tot = cnt[wk]
                    emitq(queue, lambda eng, wk=wk, tot=tot: eng.wait_ge(
                        sem[wk], tot))
                else:
                    raise AssertionError(f"cross-queue raw wait {w} on {queue}")
            else:
                wait(queue, w)
        cnt[key] += 16
        tgt = cnt[key]
        emitq(queue, lambda eng, out=out, in_=in_, key=key:
              eng.dma_start(out=out, in_=in_).then_inc(sem[key], 16))
        return (key, tgt)

    def qgate(queue):
        qk = QKEY[queue]
        gk = "gds" if queue == "sync" else "gdg"
        tot = cnt[qk]
        emitq(queue, lambda eng, qk=qk, tot=tot: eng.wait_ge(sem[qk], tot))
        cnt[gk] += 1
        tgt = cnt[gk]
        emitq(queue, lambda eng, gk=gk: eng.sem_inc(sem[gk], 1))
        return (gk, tgt)

    readers = {}

    def note_read(bufkey, ms):
        if ms is not None:
            readers.setdefault(bufkey, []).append(ms)

    def claim(e, bufkey):
        engs = e if isinstance(e, (list, tuple)) else [e]
        for ms in readers.pop(bufkey, []):
            for ee in engs:
                wait(ee, ms)

    def vdrain():
        emitq("vector", lambda eng: eng.drain())

    bank_free = {}

    def bank_wait(bank):
        if id(bank) in bank_free:
            wait("tensor", bank_free[id(bank)])

    def bank_set(bank, ms):
        bank_free[id(bank)] = ms

    def Abf(off, p0, np_, cols):
        return A[p0:p0 + np_, off:off + cols]

    def Af32(off, p0, np_, cols):
        assert off % 2 == 0
        return A.ap().bitcast(F32)[p0:p0 + np_, off // 2: off // 2 + cols]

    SO = {}
    _so = [0]
    for nm, n in [("bq", 8), ("bk8", 8), ("bv", 8), ("bo", 8), ("negc", 16),
                  ("srow", 1), ("inv", 1), ("tmp1", 1), ("mq2", 16), ("mk2", 16),
                  ("mr2", 1), ("mrrep", 1), ("hsplit", 2), ("mq2p", 8), ("mk2p", 8), ("tmp2", 1), ("krow", 16), ("crow", 16), ("negR", 16),
                  ("ck", 96), ("mact", 96), ("hist", 96), ("mnew", 96),
                  ("cmb", 96), ("cmpb", 96), ("lo", 1), ("hi", 1), ("tt", 1),
                  ("cntp", 7), ("w8", 1), ("tj", 1), ("cge", 7), ("rr", 1), ("done", 2), ("dnew", 1), ("drep", 2),
                  ("nbp", 1), ("nbs", 1), ("novs", 1), ("pool", 8), ("y1", 2),
                  ("y1b", 1), ("one_m_d", 1)]:
        SO[nm] = _so[0]
        _so[0] += n
    assert _so[0] <= SMALL_COLS

    def Sv(name, np_=128, cols=1, p0=0):
        o = SO[name]
        return S[p0:p0 + np_, o:o + cols]

    def col_block(wd, col0, width, nchunk, row_len):
        return AP(wd, col0, [[row_len, 128], [128 * row_len, nchunk], [1, width]])

    dma("sync", onesf_s[:], onesf_d[:])
    dma("sync", identb_s[:], identb_d[:])
    dma("gpsimd", rextT_s[0:64, :], rextT_d[:])
    dma("gpsimd", rextT_s[64:128, :], rextT_d[:])
    for cc in range(NC):
        dma("gpsimd", Abf(R0 + cc * 2048, 0, 128, 2048),
            xT_d[cc * 128:(cc + 1) * 128, :])
    g_x = qgate("gpsimd")
    for srcd, dstd in [(wq_d, wq_bf), (wk_d, wk_bf), (wv_d, wv_bf),
                       (wo_d, wo_bf)]:
        dma("gpsimd", dstd[:, :], srcd[:, :])
    g_cast_qkv = qgate("gpsimd")
    for srcd, dstd in [(wmT_d, wm_bf), (wpT_d, wp_bf), (wc2T_d, wc2_bf)]:
        dma("gpsimd", dstd[:, :], srcd[:, :])
    g_cast_ffn = qgate("gpsimd")

    ms_bias = {}
    for nm, dd in [("bq", bq_d), ("bk8", bk_d), ("bv", bv_d), ("bo", bo_d)]:
        dma("sync", Sv(nm, 128, 8), dd[:, :])
    g_const = qgate("sync")
    for nm in ("bq", "bk8", "bv", "bo"):
        ms_bias[nm] = g_const
    wait("vector", g_const)
    ms_bias["bk8"] = op("vector", lambda v: v.tensor_scalar(
        out=Sv("bk8", 128, 8), in0=Sv("bk8", 128, 8), scalar1=0.125, scalar2=None,
        op0=ALU.mult), inc=True)

    ring = [ps0, ps1, ps4, ps5]
    _ri = [0]

    def rbank():
        b = ring[_ri[0] % 4]
        _ri[0] += 1
        return b

    RO_wtile = [R4 + 0, R4 + 1024]
    RO_bq = [R4 + 2048 + i * 2176 for i in range(4)]
    RO_band = [R4 + 10752, R4 + 12800]
    RO_attn = [R5 + 0, R5 + 2048]
    RO_attnT = [R5 + 4096, R5 + 6144]
    RO_sc = [R5 + 8192, R5 + 10240]
    RO_sq = R5 + 8192

    def proj2(wd, bias_name, scale, dstR, vmode=False):
        wbf = {id(wq_d): wq_bf, id(wk_d): wk_bf, id(wv_d): wv_bf}[id(wd)]
        for m in range(NC):
            claim("sync", ("wtile", m % 2))
            wt = Abf(RO_wtile[m % 2], 0, 128, 1024)
            dma("sync", wt.rearrange("p (c q) -> p c q", c=8),
                col_block(wbf, m * 128, 128, 8, D),
                waits=[g_cast_qkv] if m == 0 else ())
            ms_w = qgate("sync")
            ms_e = None
            for tg in range(TG):
                bank = rbank()
                bank_wait(bank)
                if tg == 0:
                    wait("tensor", ms_w)
                    if wd is wq_d:
                        wait("tensor", g_x)
                def mk(cc, tg=tg, bank=bank, wt=wt):
                    def f(t, cc=cc, tg=tg, bank=bank, wt=wt):
                        return t.matmul(
                            bank[:, 0:512], wt[:, cc * 128:(cc + 1) * 128],
                            Abf(R0 + cc * 2048 + tg * 512, 0, 128, 512),
                            start=(cc == 0), stop=(cc == NC - 1))
                    return f
                for cc in range(NC - 1):
                    op("tensor", mk(cc))
                ms_mm = op("tensor", mk(NC - 1), inc=True)
                note_read(("wtile", m % 2), ms_mm)
                bias = S[:, SO[bias_name] + m: SO[bias_name] + m + 1]
                wait("scalar", ms_bias[bias_name])
                wait("scalar", ms_mm)
                if vmode:
                    claim("scalar", ("vstage", m % 2, tg))
                    outap = Abf(RO_attn[m % 2] + tg * 512, 0, 128, 512)
                else:
                    outap = Abf(dstR + m * 2048 + tg * 512, 0, 128, 512)
                def ev(s_, outap=outap, bank=bank, bias=bias, scale=scale):
                    return s_.activation(out=outap, in_=bank[:, 0:512],
                                         func=ACT.Identity, bias=bias, scale=scale)
                ms_e = op("scalar", ev, inc=True)
                bank_set(bank, ms_e)
            if vmode:
                wait("tensor", g_const)
                for half in range(2):
                    ptile = [pstrA, pstrB][half]
                    bank_wait(ptile)
                    wait("tensor", ms_e)
                    ms_tt = None
                    for k2 in range(8):
                        kt2 = half * 8 + k2
                        def tr(t, k2=k2, kt2=kt2, m=m, ptile=ptile):
                            return t.transpose(
                                ptile[:, k2 * 128:(k2 + 1) * 128],
                                Abf(RO_attn[m % 2] + kt2 * 128, 0, 128, 128),
                                identb_s[:])
                        if k2 < 7:
                            op("tensor", tr)
                        else:
                            ms_tt = op("tensor", tr, inc=True)
                    for tgx in (0, 1, 2, 3):
                        note_read(("vstage", m % 2, tgx), ms_tt)
                    wait("scalar", ms_tt)
                    def evt(s_, m=m, half=half, ptile=ptile):
                        return s_.activation(
                            out=Abf(R3 + m * 2048 + half * 1024, 0, 128, 1024),
                            in_=ptile[:, 0:1024], func=ACT.Identity)
                    ms_et = op("scalar", evt, inc=True)
                    bank_set(ptile, ms_et)

    proj2(wq_d, "bq", 1.0, R1)
    proj2(wk_d, "bk8", 0.125, R2)
    proj2(wv_d, "bv", 1.0, None, vmode=True)
    ms_proj_done = ("s", cnt["s"])

    sqf = Af32(RO_sq, 0, 128, 2048)
    wait("vector", g_x)
    wait("vector", ms_proj_done)
    op("vector", lambda v: v.memset(Sv("hsplit", 128, 2), 0.0))
    vdrain()
    op("vector", lambda v: v.memset(Sv("hsplit", 64, 1, 0), 1.0))
    ms_hs = op("vector", lambda v: v.memset(
        S[64:128, SO["hsplit"] + 1:SO["hsplit"] + 2], 1.0), inc=True)
    vdrain()

    first = [True]
    for half in range(2):
        sl = rextT_s[0:64, half * 2048:(half + 1) * 2048]
        sqv = Af32(RO_sq, 0, 64, 2048)
        ms_sq = op("vector", lambda v, sl=sl, sqv=sqv: v.tensor_tensor(
            out=sqv, in0=sl, in1=sl, op=ALU.mult), inc=True)
        for tg in range(TG):
            bank_wait(ps6)
            wait("tensor", ms_sq)
            wait("tensor", g_const)
            def mmf(t, tg=tg, sqv=sqv):
                return t.matmul(ps6[0:1, 0:512], onesf_s[0:64, 0:1],
                                sqv[:, tg * 512:(tg + 1) * 512],
                                start=True, stop=True)
            ms_t = op("tensor", mmf, inc=True)
            wait("vector", ms_t)
            ms_r = op("vector", lambda v: v.tensor_reduce(
                out=Sv("tmp1", 1, 1), in_=ps6[0:1, 0:512], axis=AX,
                op=ALU.max), inc=True)
            bank_set(ps6, ms_r)
            vdrain()
            if first[0]:
                op("vector", lambda v: v.tensor_copy(out=Sv("mr2", 1, 1),
                                                     in_=Sv("tmp1", 1, 1)))
                first[0] = False
            else:
                op("vector", lambda v: v.tensor_tensor(
                    out=Sv("mr2", 1, 1), in0=Sv("mr2", 1, 1),
                    in1=Sv("tmp1", 1, 1), op=ALU.max))
            vdrain()
    for which, srcR, dst in [(0, R1, "mq2p"), (1, R2, "mk2p")]:
        for cc in range(NC):
            sl = Abf(srcR + cc * 2048, 0, 128, 2048)
            ms_sq = op("vector", lambda v, sl=sl: v.tensor_tensor(
                out=sqf, in0=sl, in1=sl, op=ALU.mult), inc=True)
            for tg in range(TG):
                bank_wait(ps6)
                wait("tensor", ms_sq)
                if cc == 0 and tg == 0:
                    wait("tensor", ms_hs)
                def mmf(t, tg=tg):
                    return t.matmul(ps6[0:2, 0:512], Sv("hsplit", 128, 2),
                                    sqf[:, tg * 512:(tg + 1) * 512],
                                    start=True, stop=True)
                ms_t = op("tensor", mmf, inc=True)
                wait("vector", ms_t)
                ms_r = op("vector", lambda v: v.tensor_reduce(
                    out=Sv("tmp2", 2, 1), in_=ps6[0:2, 0:512], axis=AX,
                    op=ALU.max), inc=True)
                bank_set(ps6, ms_r)
                vdrain()
                col = S[0:2, SO[dst] + cc:SO[dst] + cc + 1]
                if tg == 0:
                    op("vector", lambda v, col=col: v.tensor_copy(
                        out=col, in_=Sv("tmp2", 2, 1)))
                else:
                    op("vector", lambda v, col=col: v.tensor_tensor(
                        out=col, in0=col, in1=Sv("tmp2", 2, 1), op=ALU.max))
                vdrain()
    ms_vfin = op("vector", lambda v: v.engine_nop(), inc=True)
    wait("scalar", ms_vfin)
    op("scalar", lambda s_: s_.activation(out=Sv("mq2p", 2, 8),
                                          in_=Sv("mq2p", 2, 8), func=ACT.Sqrt))
    op("scalar", lambda s_: s_.activation(out=Sv("mk2p", 2, 8),
                                          in_=Sv("mk2p", 2, 8), func=ACT.Sqrt))
    ms_sq3 = op("scalar", lambda s_: s_.activation(
        out=Sv("mr2", 1, 1), in_=Sv("mr2", 1, 1), func=ACT.Sqrt), inc=True)
    bank_wait(ps7)
    wait("tensor", ms_sq3)
    ms_bmr = op("tensor", lambda t: t.matmul(
        ps7[0:2, 0:1], onesf_s[0:1, 0:2], Sv("mr2", 1, 1),
        start=True, stop=True), inc=True)
    wait("vector", ms_bmr)
    ms_mrr = op("vector", lambda v: v.tensor_copy(out=Sv("mrrep", 2, 1),
                                                  in_=ps7[0:2, 0:1]), inc=True)
    bank_set(ps7, ms_mrr)
    vdrain()
    op("vector", lambda v: v.tensor_scalar(
        out=Sv("krow", 2, 8), in0=Sv("mk2p", 2, 8), scalar1=Sv("mrrep", 2, 1),
        scalar2=None, op0=ALU.add))
    vdrain()
    op("vector", lambda v: v.tensor_tensor(
        out=Sv("crow", 2, 8), in0=Sv("mq2p", 2, 8), in1=Sv("krow", 2, 8),
        op=ALU.mult))
    vdrain()
    op("vector", lambda v: v.tensor_scalar(
        out=Sv("crow", 2, 8), in0=Sv("crow", 2, 8), scalar1=-1.0,
        scalar2=None, op0=ALU.mult))
    vdrain()
    ms_cr = op("vector", lambda v: v.engine_nop(), inc=True)
    dma("sync", bounce_d[:, :], S[1:2, SO["crow"]:SO["crow"] + 8],
        waits=[ms_cr])
    dma("sync", S[0:1, SO["mk2p"]:SO["mk2p"] + 8], bounce_d[:, :],
        waits=[("ds", 0)])
    g_cr = qgate("sync")
    wait("vector", g_cr)
    op("vector", lambda v: v.memset(Sv("negR", 1, 16), 0.0))
    vdrain()
    negR_even = AP(S, SO["negR"], [[SMALL_COLS, 1], [2, 8]])
    negR_odd = AP(S, SO["negR"] + 1, [[SMALL_COLS, 1], [2, 8]])
    op("vector", lambda v: v.tensor_copy(out=negR_even, in_=Sv("crow", 1, 8)))
    ms_nR = op("vector", lambda v: v.tensor_copy(
        out=negR_odd, in_=S[0:1, SO["mk2p"]:SO["mk2p"] + 8]), inc=True)
    vdrain()
    bank_wait(ps7)
    wait("tensor", ms_nR)
    ms_bcn = op("tensor", lambda t: t.matmul(
        ps7[:, 0:16], onesf_s[0:1, :], Sv("negR", 1, 16),
        start=True, stop=True), inc=True)
    wait("vector", ms_bcn)
    ms_negc = op("vector", lambda v: v.tensor_copy(
        out=S[:, SO["negc"]:SO["negc"] + 16], in_=ps7[:, 0:16]), inc=True)
    bank_set(ps7, ms_negc)

    def finish(extra_out_ms):
        tot_ds = cnt["ds"]
        emitq("sync", lambda eng, tot=tot_ds: eng.wait_ge(sem["ds"], tot))
        tot_dg = cnt["dg"]
        emitq("gpsimd", lambda eng, tot=tot_dg: eng.wait_ge(sem["dg"], tot))
        for e in ["tensor", "vector", "scalar", "sync", "gpsimd"]:
            fns = q[e]
            if fns:
                getattr(blk, e)(lambda eng, fns=fns: [f(eng) for f in fns])
        es.close()
        return nc, dbg

    if stage == "qkv":
        outs = []
        for cc in range(NC):
            outs.append(dma("sync", dbg["d_qt"][cc * 128:(cc + 1) * 128, :],
                            Abf(R1 + cc * 2048, 0, 128, 2048),
                            waits=[ms_proj_done]))
            outs.append(dma("sync", dbg["d_kt"][cc * 128:(cc + 1) * 128, :],
                            Abf(R2 + cc * 2048, 0, 128, 2048)))
            outs.append(dma("sync", dbg["d_vt"][cc * 128:(cc + 1) * 128, :],
                            Abf(R3 + cc * 2048, 0, 128, 2048)))
        outs.append(dma("sync", out_d[:, :], xT_d[:, :]))
        return finish(outs)

    def slot_base(h, qt):
        return (h % 2) * QT * SLOT + qt * SLOT

    wgate = {}
    rgate = {}


    def emit_bprime_qt(h, qt):
        _bprime_core(h, qt, interleave=None)

    def _bprime_core(h, qt, interleave):
        cchunk, poff = h // 2, 64 * (h % 2)
        lhs = Abf(R1 + cchunk * 2048 + qt * 128, poff, 64, 128)
        bqbuf = Abf(RO_bq[qt % 4], 0, 128, 2176)
        w0 = 1920 - qt * 128
        widths = [512, 512, 512, 512, 128]
        off = 0
        ms_e_s = ms_e_v = None
        for wi, wdt in enumerate(widths):
            bank = rbank()
            bank_wait(bank)
            if wi == 0 and h == 0 and qt == 0:
                wait("tensor", g_x)
            def mmf(t, lhs=lhs, w0=w0, off=off, wdt=wdt, bank=bank, poff=poff):
                return t.matmul(bank[:, 0:wdt], lhs,
                                rextT_s[poff:poff + 64,
                                        w0 + off:w0 + off + wdt],
                                start=True, stop=True)
            ms_t = op("tensor", mmf, inc=True)
            if interleave is not None:
                nxt = next(interleave, None)
                if nxt is not None:
                    nxt()
            ev_eng = "scalar" if wi < 2 else "vector"
            wait(ev_eng, ms_t)
            if wi == 0:
                claim(("scalar", "vector"), ("bq", qt % 4))
            if ev_eng == "scalar":
                def ev(s_, off=off, wdt=wdt, bank=bank, bqbuf=bqbuf):
                    return s_.activation(out=bqbuf[:, off:off + wdt],
                                         in_=bank[:, 0:wdt], func=ACT.Identity)
            else:
                def ev(v, off=off, wdt=wdt, bank=bank, bqbuf=bqbuf):
                    return v.tensor_copy(out=bqbuf[:, off:off + wdt],
                                         in_=bank[:, 0:wdt])
            ms_e = op(ev_eng, ev, inc=True)
            if ev_eng == "scalar":
                ms_e_s = ms_e
            else:
                ms_e_v = ms_e
            bank_set(bank, ms_e)
            off += wdt
        base = slot_base(h, qt)
        diag = AP(skew_d, base, [[SW + 1, 128], [1, 2176]])
        dma("sync", diag, bqbuf,
            waits=[m for m in (ms_e_s, ms_e_v) if m is not None]
            + [m for m in readers.pop(("skslot", h % 2, qt), [])])
        if qt % 4 == 3:
            g = qgate("sync")
            for q2 in range(qt - 3, qt + 1):
                note_read(("bq", q2 % 4), g)
            wgate[(h, qt // 4)] = g

    sc_state = {}

    def scores_part1(h, qt, bprime_next=None):
        cchunk, poff = h // 2, 64 * (h % 2)
        negc = S[:, SO["negc"] + h:SO["negc"] + h + 1]
        lhs = Abf(R1 + cchunk * 2048 + qt * 128, poff, 64, 128)
        scb = Abf(RO_sc[qt % 2], 0, 128, 2048)
        attnb = Abf(RO_attn[qt % 2], 0, 128, 2048)
        bandb = Abf(RO_band[qt % 2], 0, 128, 2048)
        if qt % 2 == 0:
            for q2 in (qt, qt + 1):
                base2 = slot_base(h, q2)
                rd = AP(skew_d, base2 + 127, [[SW, 128], [1, 2048]])
                dma("sync", Abf(RO_band[q2 % 2], 0, 128, 2048), rd,
                    waits=([wgate[(h, q2 // 4)]] if q2 % 4 == 0 else [])
                    + [m for m in readers.pop(("band", q2 % 2), [])])
            g = qgate("sync")
            rgate[(h, qt // 2)] = g
            note_read(("skslot", h % 2, qt), g)
            note_read(("skslot", h % 2, qt + 1), g)

        ms_add = None

        def qk_fill(tg):
            nonlocal ms_add
            bank = rbank()
            bank_wait(bank)
            def mmf(t, lhs=lhs, tg=tg, bank=bank, cchunk=cchunk, poff=poff):
                return t.matmul(
                    bank[:, 0:512], lhs,
                    Abf(R2 + cchunk * 2048 + tg * 512, poff, 64, 512),
                    start=True, stop=True)
            ms_t = op("tensor", mmf, inc=True)
            wait("vector", ms_t)
            if tg == 0:
                claim("vector", ("sc", qt % 2))
                wait("vector", rgate[(h, qt // 2)])
            def cl(v, tg=tg, bank=bank, scb=scb, bandb=bandb):
                return v.tensor_tensor(out=scb[:, tg * 512:(tg + 1) * 512],
                                       in0=bank[:, 0:512],
                                       in1=bandb[:, tg * 512:(tg + 1) * 512],
                                       op=ALU.add)
            ms_c = op("vector", cl, inc=True)
            bank_set(bank, ms_c)
            ms_add = ms_c

        if bprime_next is not None:
            _bprime_core(bprime_next[0], bprime_next[1],
                         iter([lambda tg=tg: qk_fill(tg) for tg in range(TG)]))
        else:
            for tg in range(TG):
                qk_fill(tg)
        note_read(("band", qt % 2), ms_add)
        if h == 0 and qt == 0:
            wait("scalar", ms_negc)
        wait("scalar", ms_add)
        claim("scalar", ("attn", qt % 2))
        ms_exp = op("scalar", lambda s_, scb=scb, attnb=attnb, negc=negc:
                    s_.activation(out=attnb, in_=scb, func=ACT.Exp, bias=negc,
                                  scale=1.0, accum_out=Sv("srow", 128, 1)),
                    inc=True)
        note_read(("sc", qt % 2), ms_exp)
        if h == 0 and qt == 0 and "d_negc" in dbg:
            dma("sync", dbg["d_negc"][:, :], S[:, SO["negc"]:SO["negc"] + 16],
                waits=[ms_exp])
            dma("sync", dbg["d_sm"][:, :], S[0:2, SO["mq2p"]:SO["mq2p"] + 64])
            dma("sync", dbg["d_srow"][:, :], Sv("srow", 128, 1))
            dma("sync", dbg["d_sc0"][:, :], scb)
            dma("sync", dbg["d_band0"][:, :], bandb)
        ms_tmp1 = op("scalar", lambda s_: s_.activation(
            out=Sv("tmp1", 128, 1), in_=Sv("srow", 128, 1),
            func=ACT.Identity, scale=1.0 + 1e-8), inc=True)
        wait("vector", ms_tmp1)
        ms_inv = op("vector", lambda v: v.reciprocal(
            out=Sv("inv", 128, 1), in_=Sv("tmp1", 128, 1)), inc=True)
        wait("gpsimd", ms_inv)
        ms_scl = op("gpsimd", lambda g, attnb=attnb: g.tensor_scalar(
            out=attnb, in0=attnb, scalar1=Sv("inv", 128, 1), scalar2=None,
            op0=ALU.mult), inc=True)
        sc_state[(h, qt)] = ms_scl

    def scores_part2(h, qt):
        cchunk, poff = h // 2, 64 * (h % 2)
        attnb = Abf(RO_attn[qt % 2], 0, 128, 2048)
        attnTb = Abf(RO_attnT[qt % 2], 0, 128, 2048)
        ms_scl = sc_state.pop((h, qt))
        ev_ms = []
        for half in range(2):
            ptile = [pstrA, pstrB][half]
            bank_wait(ptile)
            if half == 0:
                wait("tensor", ms_scl)
            ms_tt = None
            for k2 in range(8):
                kt2 = half * 8 + k2
                def tr(t, k2=k2, kt2=kt2, attnb=attnb, ptile=ptile):
                    return t.transpose(ptile[:, k2 * 128:(k2 + 1) * 128],
                                       attnb[:, kt2 * 128:kt2 * 128 + 128],
                                       identb_s[:])
                if k2 < 7:
                    op("tensor", tr)
                else:
                    ms_tt = op("tensor", tr, inc=True)
            if half == 1:
                note_read(("attn", qt % 2), ms_tt)
            ev_eng = "scalar" if half == 0 else "vector"
            wait(ev_eng, ms_tt)
            if half == 0:
                claim(("scalar", "vector"), ("attnT", qt % 2))
            if ev_eng == "scalar":
                def evt(s_, half=half, attnTb=attnTb, ptile=ptile):
                    return s_.activation(
                        out=attnTb[:, half * 1024:half * 1024 + 1024],
                        in_=ptile[:, 0:1024], func=ACT.Identity)
            else:
                def evt(v, half=half, attnTb=attnTb, ptile=ptile):
                    return v.tensor_copy(
                        out=attnTb[:, half * 1024:half * 1024 + 1024],
                        in_=ptile[:, 0:1024])
            ms_et = op(ev_eng, evt, inc=True)
            bank_set(ptile, ms_et)
            ev_ms.append(ms_et)
        bank = [ps6, ps7][qt % 2]
        bank_wait(bank)
        wait("tensor", ev_ms[0])
        wait("tensor", ev_ms[1])
        ms_ctx = None
        for kt2 in range(KT):
            def mmf(t, kt2=kt2, bank=bank, attnTb=attnTb, cchunk=cchunk,
                    poff=poff):
                return t.matmul(
                    bank[0:64, 0:128],
                    Abf(R3 + cchunk * 2048 + kt2 * 128 + poff, 0, 128, 64),
                    attnTb[:, kt2 * 128:kt2 * 128 + 128],
                    start=(kt2 == 0), stop=(kt2 == KT - 1))
            if kt2 < KT - 1:
                op("tensor", mmf)
            else:
                ms_ctx = op("tensor", mmf, inc=True)
        note_read(("attnT", qt % 2), ms_ctx)
        wait("vector", ms_ctx)
        def evc(v, bank=bank, qt=qt, cchunk=cchunk, poff=poff):
            return v.tensor_copy(
                out=Abf(R0 + cchunk * 2048 + qt * 128, poff, 64, 128),
                in_=bank[0:64, 0:128])
        ms_ec = op("vector", evc, inc=True)
        bank_set(bank, ms_ec)

    for qt in range(QT):
        emit_bprime_qt(0, qt)
    seq = [(h, qt) for h in range(H) for qt in range(QT)]
    prev = None
    for idx, (h, qt) in enumerate(seq):
        scores_part1(h, qt,
                     bprime_next=((h + 1, qt) if h + 1 < H else None))
        if prev is not None:
            scores_part2(*prev)
        prev = (h, qt)
    scores_part2(*prev)
    ms_ctx_all = ("v", cnt["v"])

    if stage == "attn":
        outs = []
        for cc in range(NC):
            outs.append(dma("sync", dbg["d_ctx"][cc * 128:(cc + 1) * 128, :],
                            Abf(R0 + cc * 2048, 0, 128, 2048), waits=[ms_ctx_all]))
        outs.append(dma("sync", out_d[:, :], xT_d[:, :]))
        return finish(outs)

    if stage == "x1":
        for cc in range(NC):
            dma("sync", dbg["d_ctx"][cc * 128:(cc + 1) * 128, :],
                Abf(R0 + cc * 2048, 0, 128, 2048), waits=[ms_ctx_all])
    YROT = [R4 + 0, R4 + 4096]
    XREROT = [R4 + 8192, R4 + 12288]
    SQROT = R5 + 0
    STATROW = R5 + 4096

    def stat_rows():
        return Af32(STATROW, 0, 1, 2048), Af32(STATROW + 4096, 0, 1, 2048)

    def emit_ln(src_fn, spill_dram, norm_out_dram, dst_bf_R, x1cast_ms_out,
                tapf=None, ybuf_fn=None):
        sumr, sqr = stat_rows()
        spill_ms = []
        for m in range(NC):
            if ybuf_fn is not None:
                ybuf = ybuf_fn(m)
            else:
                ybuf = Af32(YROT[m % 2], 0, 128, 2048)
                claim(("vector", "scalar"), ("yrot", m % 2))
            ms_y = src_fn(m, ybuf)
            wait("scalar", ms_y)
            claim("scalar", ("sqrot",))
            ms_sq = op("scalar", lambda s_, ybuf=ybuf: s_.activation(
                out=Af32(SQROT, 0, 128, 2048), in_=ybuf, func=ACT.Square), inc=True)
            for which, srcf, msdep, _bx in [
                    (0, ybuf, ms_y, None),
                    (1, Af32(SQROT, 0, 128, 2048), ms_sq, None)]:
                row = sumr if which == 0 else sqr
                for tg in range(TG):
                    bankx = rbank()
                    bank_wait(bankx)
                    wait("tensor", msdep)
                    def mmf(t, srcf=srcf, tg=tg, bankx=bankx):
                        return t.matmul(bankx[0:1, 0:512], onesf_s[:, 0:1],
                                        srcf[:, tg * 512:(tg + 1) * 512],
                                        start=True, stop=True)
                    ms_t = op("tensor", mmf, inc=True)
                    if which == 0:
                        note_read(("yrot", m % 2), ms_t)
                    else:
                        note_read(("sqrot",), ms_t)
                    wait("vector", ms_t)
                    if m == 0:
                        def ac(v, row=row, tg=tg, bankx=bankx):
                            return v.tensor_copy(out=row[:, tg * 512:(tg + 1) * 512],
                                                 in_=bankx[0:1, 0:512])
                    else:
                        def ac(v, row=row, tg=tg, bankx=bankx):
                            return v.tensor_tensor(
                                out=row[:, tg * 512:(tg + 1) * 512],
                                in0=row[:, tg * 512:(tg + 1) * 512],
                                in1=bankx[0:1, 0:512], op=ALU.add)
                    ms_a = op("vector", ac, inc=True)
                    vdrain()
                    bank_set(bankx, ms_a)
            if ybuf_fn is None:
                dma("sync", spill_dram[m * 128:(m + 1) * 128, :], ybuf,
                    waits=[ms_sq])
                g_sp = qgate("sync")
                note_read(("yrot", m % 2), g_sp)
                spill_ms.append(g_sp)
        vdrain()
        op("vector", lambda v: v.tensor_scalar(out=sumr, in0=sumr, scalar1=1.0 / D,
                                               scalar2=None, op0=ALU.mult))
        vdrain()
        op("vector", lambda v: v.tensor_scalar(out=sqr, in0=sqr, scalar1=1.0 / D,
                                               scalar2=None, op0=ALU.mult))
        vdrain()
        claim("vector", ("sqrot",))
        op("vector", lambda v: v.tensor_tensor(out=Af32(SQROT, 0, 1, 2048),
                                               in0=sumr, in1=sumr, op=ALU.mult))
        vdrain()
        op("vector", lambda v: v.tensor_tensor(out=sqr, in0=sqr,
                                               in1=Af32(SQROT, 0, 1, 2048),
                                               op=ALU.subtract))
        vdrain()
        ms_var = op("vector", lambda v: v.tensor_scalar(
            out=sqr, in0=sqr, scalar1=EPS_LN, scalar2=None, op0=ALU.add), inc=True)
        vdrain()
        note_read(("sqrot",), ms_var)
        wait("scalar", ms_var)
        ms_sqt = op("scalar", lambda s_: s_.activation(out=sqr, in_=sqr,
                                                       func=ACT.Sqrt), inc=True)
        wait("vector", ms_sqt)
        ms_rstd = op("vector", lambda v: v.reciprocal(out=sqr, in_=sqr), inc=True)
        vdrain()
        if "d_stat" in dbg and spill_dram.name == "y_spill":
            dma("sync", dbg["d_stat"][0], sumr, waits=[ms_rstd])
            dma("sync", dbg["d_stat"][1], sqr)
        out_ms = []
        for m in range(NC):
            if ybuf_fn is not None:
                yre = ybuf_fn(m)
                ms_re = None
            else:
                yre = Af32(YROT[m % 2], 0, 128, 2048)
                dma("sync", yre, spill_dram[m * 128:(m + 1) * 128, :],
                    waits=[m2 for m2 in readers.pop(("yrot", m % 2), [])])
                ms_re = qgate("sync")
            x1b = Af32(XREROT[m % 2], 0, 128, 2048)
            claim("vector", ("xrerot", m % 2))
            ms_mu = None
            for tg in range(TG):
                bank_wait(ps6)
                bank_wait(ps7)
                wait("tensor", ms_rstd)
                ms_m = op("tensor", lambda t, tg=tg: t.matmul(
                    ps6[:, 0:512], onesf_s[0:1, :],
                    sumr[:, tg * 512:(tg + 1) * 512], start=True, stop=True),
                    inc=True)
                ms_r2 = op("tensor", lambda t, tg=tg: t.matmul(
                    ps7[:, 0:512], onesf_s[0:1, :],
                    sqr[:, tg * 512:(tg + 1) * 512], start=True, stop=True),
                    inc=True)
                wait("vector", ms_r2)
                if ms_re is not None:
                    wait("vector", ms_re)
                def sub(v, yre=yre, tg=tg, x1b=x1b):
                    return v.tensor_tensor(out=x1b[:, tg * 512:(tg + 1) * 512],
                                           in0=yre[:, tg * 512:(tg + 1) * 512],
                                           in1=ps6[:, 0:512], op=ALU.subtract)
                op("vector", sub)
                vdrain()
                def mul(v, tg=tg, x1b=x1b):
                    return v.tensor_tensor(out=x1b[:, tg * 512:(tg + 1) * 512],
                                           in0=x1b[:, tg * 512:(tg + 1) * 512],
                                           in1=ps7[:, 0:512], op=ALU.mult)
                ms_mu = op("vector", mul, inc=True)
                bank_set(ps6, ms_mu)
                bank_set(ps7, ms_mu)
            note_read(("yrot", m % 2), ms_mu)
            dma("sync", norm_out_dram[m * 128:(m + 1) * 128, :], x1b,
                waits=[ms_mu])
            if tapf is not None:
                dma("sync", tapf[m * 128:(m + 1) * 128, :], x1b)
            ms_o = qgate("sync")
            note_read(("xrerot", m % 2), ms_o)
            out_ms.append(ms_o)
            if dst_bf_R is not None:
                wait("scalar", ms_mu)
                ms_cst = op("scalar", lambda s_, x1b=x1b, m=m: s_.activation(
                    out=Abf(dst_bf_R + m * 2048, 0, 128, 2048), in_=x1b,
                    func=ACT.Identity), inc=True)
                note_read(("xrerot", m % 2), ms_cst)
                x1cast_ms_out.append(ms_cst)
        return out_ms, spill_ms

    def ybuf3(m):
        if m < 4:
            return Af32(R1 + m * 4096, 0, 128, 2048)
        return Af32(R3 + (m - 4) * 4096, 0, 128, 2048)

    def attn_src(m, ybuf):
        claim("sync", ("wotile", m % 2))
        wt = Abf(R2 + (m % 2) * 1024, 0, 128, 1024)
        dma("sync", wt.rearrange("p (c q) -> p c q", c=8),
            col_block(wo_bf, m * 128, 128, 8, D),
            waits=([ms_ctx_all] if m < 2 else []) + ([g_cast_qkv] if m == 0 else []))
        ms_w = qgate("sync")
        xre = Af32(XREROT[m % 2], 0, 128, 2048)
        dma("sync", xre, xT_d[m * 128:(m + 1) * 128, :],
            waits=[m2 for m2 in readers.pop(("xrerot", m % 2), [])]
            + ([ms_ctx_all] if m < 2 else []))
        ms_x = qgate("sync")
        ms_last = None
        for tg in range(TG):
            bank = rbank()
            bank_wait(bank)
            if tg == 0:
                wait("tensor", ms_w)
                wait("tensor", ms_ctx_all)
            def mk(dd, tg=tg, bank=bank, wt=wt):
                def f(t, dd=dd, tg=tg, bank=bank, wt=wt):
                    return t.matmul(bank[:, 0:512], wt[:, dd * 128:(dd + 1) * 128],
                                    Abf(R0 + dd * 2048 + tg * 512, 0, 128, 512),
                                    start=(dd == 0), stop=(dd == NC - 1))
                return f
            for dd in range(NC - 1):
                op("tensor", mk(dd))
            ms_mm = op("tensor", mk(NC - 1), inc=True)
            note_read(("wotile", m % 2), ms_mm)
            bias = S[:, SO["bo"] + m:SO["bo"] + m + 1]
            wait("scalar", ms_bias["bo"])
            wait("scalar", ms_mm)
            def ev(s_, tg=tg, bank=bank, ybuf=ybuf, bias=bias):
                return s_.activation(out=ybuf[:, tg * 512:(tg + 1) * 512],
                                     in_=bank[:, 0:512], func=ACT.Identity,
                                     bias=bias, scale=1.0)
            ms_last = op("scalar", ev, inc=True)
            bank_set(bank, ms_last)
        wait("vector", ms_last)
        wait("vector", ms_x)
        ms_y = op("vector", lambda v, ybuf=ybuf, xre=xre: v.tensor_tensor(
            out=ybuf, in0=ybuf, in1=xre, op=ALU.add), inc=True)
        note_read(("xrerot", m % 2), ms_y)
        return ms_y

    x1_cast_ms = []
    x1_out_ms, _ = emit_ln(attn_src, y_spill, x1_spill, R2, x1_cast_ms,
                           tapf=dbg.get("d_x1"), ybuf_fn=ybuf3)

    if stage == "x1":
        outs = [dma("sync", out_d[:, :], x1_spill[:, :],
                    waits=[x1_out_ms[-1]]),
                dma("sync", dbg["d_y"][:, :], y_spill[:, :])]
        return finish(outs)

    def xacc(cc):
        if cc < 4:
            return Af32(R0 + cc * 4096, 0, 128, 2048)
        return Af32(R1 + (cc - 4) * 4096, 0, 128, 2048)

    def xcur(cc):
        return Abf(R2 + cc * 2048, 0, 128, 2048)

    def hblk(ffl):
        return Abf(R3 + ffl * 2048, 0, 128, 2048)

    WM_OFF = R4
    WP_OFF = R4 + 8192
    ROTF = R5 + 0
    WC1_OFF = R5 + 4096

    wc1_s = Af32(WC1_OFF, 0, 128, 2048)
    dma("sync", wc1_s.rearrange("p (c q) -> p c q", c=8),
        col_block(wc1T_d, 0, 256, 8, 256),
        waits=[("t", cnt["t"]), ("v", cnt["v"])])
    ms_wc1 = qgate("sync")
    wait("sync", ("v", cnt["v"]))
    wait("sync", ("s", cnt["s"]))

    op("vector", lambda v: v.memset(S[:, SO["mact"]:SO["mact"] + 32], 1.0))
    ms_mbase = op("vector", lambda v: v.memset(
        S[:, SO["mact"] + 32:SO["mact"] + 96], 0.0), inc=True)
    vdrain()
    op("vector", lambda v: v.tensor_copy(out=S[:, SO["hist"]:SO["hist"] + 96],
                                         in_=S[:, SO["mact"]:SO["mact"] + 96]))
    vdrain()
    op("vector", lambda v: v.memset(S[:, SO["done"]:SO["done"] + 2], 0.0))
    op("vector", lambda v: v.memset(S[:, SO["drep"]:SO["drep"] + 2], 0.0))

    mask_ready_ms = {0: None, 1: None}
    nov_done_ms = {}
    bis_tensor_queue = []
    bis_gp = {}

    def bis_prepass(sm, ms_ck):
        wait("vector", ms_ck)
        op("vector", lambda v: v.memset(Sv("lo", 128, 1), -1024.0))
        op("vector", lambda v: v.memset(Sv("hi", 128, 1), 1024.0))
        ps6_free = bank_free.get(id(ps6))
        for i in range(NIT):
            vdrain()
            op("vector", lambda v: v.tensor_tensor(out=Sv("w8", 128, 1),
                                                   in0=Sv("hi", 128, 1),
                                                   in1=Sv("lo", 128, 1),
                                                   op=ALU.subtract))
            vdrain()
            op("vector", lambda v: v.tensor_scalar(out=Sv("w8", 128, 1),
                                                   in0=Sv("w8", 128, 1),
                                                   scalar1=1.0 / (NPROBE + 1.0),
                                                   scalar2=None, op0=ALU.mult))
            vdrain()
            ms_cnt = None
            for jp in range(NPROBE):
                op("vector", lambda v, jp=jp: v.tensor_scalar(
                    out=Sv("tj", 128, 1), in0=Sv("w8", 128, 1),
                    scalar1=float(jp + 1), scalar2=None, op0=ALU.mult))
                vdrain()
                op("vector", lambda v: v.tensor_tensor(
                    out=Sv("tj", 128, 1), in0=Sv("tj", 128, 1),
                    in1=Sv("lo", 128, 1), op=ALU.add))
                vdrain()
                op("vector", lambda v: v.tensor_scalar(
                    out=S[:, SO["cmpb"]:SO["cmpb"] + 96],
                    in0=S[:, SO["ck"]:SO["ck"] + 96], scalar1=Sv("tj", 128, 1),
                    scalar2=None, op0=ALU.is_ge))
                vdrain()
                red = lambda v, jp=jp: v.tensor_reduce(
                    out=S[:, SO["cntp"] + jp:SO["cntp"] + jp + 1],
                    in_=S[:, SO["cmpb"]:SO["cmpb"] + 96], axis=AX, op=ALU.add)
                if jp == NPROBE - 1:
                    ms_cnt = op("vector", red, inc=True, sem_key="vb")
                else:
                    op("vector", red)
            cnt["tb"] += 1
            tbi = cnt["tb"]
            def tclos(ms_cnt=ms_cnt, ps6_free=ps6_free):
                wait("tensor", ms_cnt)
                if ps6_free is not None:
                    wait("tensor", ps6_free)
                emitq("tensor", lambda t: t.matmul(
                    ps6[:, 0:NPROBE], onesf_s[:], Sv("cntp", 128, NPROBE),
                    start=True, stop=True).then_inc(sem["tb"], 1))
            bis_tensor_queue.append(tclos)
            wait("vector", ("tb", tbi))
            ms_ev = op("vector", lambda v: v.tensor_copy(
                out=Sv("cntp", 128, NPROBE), in_=ps6[:, 0:NPROBE]), inc=True)
            ps6_free = ms_ev
            vdrain()
            op("vector", lambda v: v.tensor_scalar(
                out=Sv("cge", 128, NPROBE), in0=Sv("cntp", 128, NPROBE),
                scalar1=float(ITERK), scalar2=None, op0=ALU.is_ge))
            vdrain()
            op("vector", lambda v: v.tensor_reduce(
                out=Sv("rr", 128, 1), in_=Sv("cge", 128, NPROBE), axis=AX,
                op=ALU.add))
            vdrain()
            op("vector", lambda v: v.tensor_tensor(
                out=Sv("rr", 128, 1), in0=Sv("rr", 128, 1), in1=Sv("w8", 128, 1),
                op=ALU.mult))
            vdrain()
            op("vector", lambda v: v.tensor_tensor(
                out=Sv("lo", 128, 1), in0=Sv("lo", 128, 1), in1=Sv("rr", 128, 1),
                op=ALU.add))
            vdrain()
            op("vector", lambda v: v.tensor_tensor(
                out=Sv("hi", 128, 1), in0=Sv("lo", 128, 1), in1=Sv("w8", 128, 1),
                op=ALU.add))
        vdrain()
        op("vector", lambda v: v.tensor_scalar(
            out=S[:, SO["mnew"]:SO["mnew"] + 96],
            in0=S[:, SO["ck"]:SO["ck"] + 96], scalar1=Sv("lo", 128, 1),
            scalar2=None, op0=ALU.is_ge))
        vdrain()
        if sm == 0:
            ms_ma = op("vector", lambda v: v.tensor_copy(
                out=S[:, SO["mact"]:SO["mact"] + 96],
                in_=S[:, SO["mnew"]:SO["mnew"] + 96]), inc=True)
        else:
            dprev = S[:, SO["drep"]:SO["drep"] + 1]
            op("vector", lambda v, dprev=dprev: v.tensor_scalar(
                out=Sv("one_m_d", 128, 1), in0=dprev, scalar1=-1.0, scalar2=1.0,
                op0=ALU.mult, op1=ALU.add))
            vdrain()
            op("vector", lambda v: v.tensor_tensor(
                out=S[:, SO["cmpb"]:SO["cmpb"] + 96],
                in0=S[:, SO["mnew"]:SO["mnew"] + 96],
                in1=S[:, SO["mact"]:SO["mact"] + 96], op=ALU.subtract))
            vdrain()
            op("vector", lambda v: v.tensor_scalar(
                out=S[:, SO["cmpb"]:SO["cmpb"] + 96],
                in0=S[:, SO["cmpb"]:SO["cmpb"] + 96],
                scalar1=Sv("one_m_d", 128, 1), scalar2=None, op0=ALU.mult))
            vdrain()
            ms_ma = op("vector", lambda v: v.tensor_tensor(
                out=S[:, SO["mact"]:SO["mact"] + 96],
                in0=S[:, SO["mact"]:SO["mact"] + 96],
                in1=S[:, SO["cmpb"]:SO["cmpb"] + 96], op=ALU.add), inc=True)
        vdrain()
        mask_ready_ms[sm] = ms_ma
        op("vector", lambda v: v.tensor_tensor(
            out=S[:, SO["cmb"]:SO["cmb"] + 96],
            in0=S[:, SO["hist"]:SO["hist"] + 96],
            in1=S[:, SO["mnew"]:SO["mnew"] + 96], op=ALU.add))
        vdrain()
        op("vector", lambda v: v.tensor_scalar(
            out=S[:, SO["cmb"]:SO["cmb"] + 96],
            in0=S[:, SO["cmb"]:SO["cmb"] + 96], scalar1=1.0, scalar2=None,
            op0=ALU.min))
        vdrain()
        op("vector", lambda v: v.tensor_tensor(
            out=S[:, SO["cmpb"]:SO["cmpb"] + 96],
            in0=S[:, SO["cmb"]:SO["cmb"] + 96],
            in1=S[:, SO["hist"]:SO["hist"] + 96], op=ALU.subtract))
        vdrain()
        ms_nb = op("vector", lambda v: v.tensor_reduce(
            out=Sv("nbp", 128, 1), in_=S[:, SO["cmpb"]:SO["cmpb"] + 96], axis=AX,
            op=ALU.add), inc=True, sem_key="vb")
        if sm == 0:
            op("vector", lambda v: v.tensor_copy(
                out=S[:, SO["hist"]:SO["hist"] + 96],
                in_=S[:, SO["cmb"]:SO["cmb"] + 96]))
        cnt["tb"] += 1
        tb_nb = cnt["tb"]
        def tclos_nb(ms_nb=ms_nb, ps6_free=ps6_free):
            wait("tensor", ms_nb)
            if ps6_free is not None:
                wait("tensor", ps6_free)
            emitq("tensor", lambda t: t.matmul(
                ps6[:, 0:1], onesf_s[:], Sv("nbp", 128, 1), start=True,
                stop=True).then_inc(sem["tb"], 1))
        bis_tensor_queue.append(tclos_nb)
        wait("vector", ("tb", tb_nb))
        ms_nbs = op("vector", lambda v: v.tensor_copy(
            out=Sv("nbs", 1, 1), in_=ps6[0:1, 0:1]), inc=True)
        bank_set(ps6, ms_nbs)
        bis_gp[sm] = ms_nbs

    def bis_nov_finish(sm):
        wait("vector", ("ccb", 16 * (sm + 1)))
        op("vector", lambda v: v.tensor_scalar(
            out=Sv("novs", 1, 1), in0=Sv("novs", 1, 1),
            scalar1=1.0 / (8.0 * ITERK), scalar2=None, op0=ALU.mult))
        vdrain()
        op("vector", lambda v: v.tensor_scalar(
            out=Sv("dnew", 1, 1), in0=Sv("novs", 1, 1), scalar1=RNOV,
            scalar2=None, op0=ALU.is_lt))
        vdrain()
        if sm == 0:
            ms_d = op("vector", lambda v: v.tensor_copy(
                out=S[0:1, SO["done"]:SO["done"] + 1], in_=Sv("dnew", 1, 1)),
                inc=True)
        else:
            ms_d = op("vector", lambda v: v.tensor_tensor(
                out=S[0:1, SO["done"] + 1:SO["done"] + 2],
                in0=S[0:1, SO["done"]:SO["done"] + 1],
                in1=Sv("dnew", 1, 1), op=ALU.max), inc=True)
        bank_wait(ps7)
        wait("tensor", ms_d)
        ms_bc = op("tensor", lambda t, sm=sm: t.matmul(
            ps7[:, 0:1], onesf_s[0:1, :],
            S[0:1, SO["done"] + sm:SO["done"] + sm + 1],
            start=True, stop=True), inc=True)
        wait("vector", ms_bc)
        ms_dr = op("vector", lambda v, sm=sm: v.tensor_copy(
            out=S[:, SO["drep"] + sm:SO["drep"] + sm + 1], in_=ps7[:, 0:1]),
            inc=True)
        vdrain()
        bank_set(ps7, ms_dr)
        nov_done_ms[sm] = ms_dr

    def bis_gpsimd(sm):
        ms_nbs = bis_gp[sm]
        ms_in = dma("gpsimd", nov_in[sm][:], Sv("nbs", 1, 1), waits=[ms_nbs])
        wait("gpsimd", ms_in)
        cnt["cc"] += 1
        tgt = cnt["cc"]
        emitq("gpsimd", lambda g, sm=sm: g.collective_compute(
            "AllReduce", ALU.add, replica_groups=[list(range(8))],
            ins=[nov_in[sm][:]], outs=[nov_out[sm][:]]).then_inc(sem["cc"], 1))
        wait("gpsimd", ("cc", tgt))
        dma("gpsimd", Sv("novs", 1, 1), nov_out[sm][:], sem_key="ccb")

    last_h_mm = {"ms": None}
    last_xnew_mm = {"ms": None}
    ck_read_ms = {"ms": None}

    def emit_step(s, rhs_ready):
        tick_iter = iter(list(bis_tensor_queue))
        bis_tensor_queue.clear()
        relu_fin = [None] * FC
        for blki in range(NBLK):
            for cc in range(NC):
                dma("sync", Abf(WM_OFF + cc * 1024, 0, 128, 1024),
                    wm_bf[cc * 128:(cc + 1) * 128,
                          blki * FBLK * 128:(blki + 1) * FBLK * 128],
                    waits=([last_h_mm["ms"]] if (cc == 0 and last_h_mm["ms"])
                           else []) + ([g_cast_ffn] if (s == 0 and blki == 0
                                                        and cc == 0) else []))
            g_wm = qgate("sync")
            if blki == 2:
                if s >= 1:
                    bis_gpsimd(s - 1)
            prev_xnew = last_xnew_mm["ms"]
            for ffl in range(FBLK):
                fc = blki * FBLK + ffl
                for tg in range(TG):
                    bank = rbank()
                    bank_wait(bank)
                    if ffl == 0 and tg == 0:
                        wait("tensor", g_wm)
                        for msr in rhs_ready:
                            wait("tensor", msr)
                    ms_mm = None
                    for cc in range(NC):
                        def f(t, cc=cc, ffl=ffl, tg=tg, bank=bank):
                            return t.matmul(
                                bank[:, 0:512],
                                Abf(WM_OFF + cc * 1024 + ffl * 128, 0, 128, 128),
                                xcur(cc)[:, tg * 512:(tg + 1) * 512],
                                start=(cc == 0), stop=(cc == NC - 1))
                        if cc == NC - 1:
                            ms_mm = op("tensor", f, inc=True)
                        else:
                            op("tensor", f)
                    last_h_mm["ms"] = ms_mm
                    wait("scalar", ms_mm)
                    if ffl == 0 and tg == 0 and prev_xnew is not None:
                        wait("scalar", prev_xnew)
                    def ev(s_, ffl=ffl, tg=tg, bank=bank):
                        return s_.activation(
                            out=hblk(ffl)[:, tg * 512:(tg + 1) * 512],
                            in_=bank[:, 0:512], func=ACT.Relu)
                    ms_e = op("scalar", ev, inc=True)
                    bank_set(bank, ms_e)
                    if tg == TG - 1:
                        relu_fin[fc] = ms_e
                if blki == 0:
                    nxt = next(tick_iter, None)
                    if nxt is not None:
                        nxt()
            if blki == 0:
                for nxt in tick_iter:
                    nxt()
                tick_iter = iter(())
            for ffl in range(FBLK):
                fc = blki * FBLK + ffl
                wait("vector", relu_fin[fc])
                if s == 0:
                    if blki == 0 and ffl == 0:
                        wait("vector", ms_mbase)
                else:
                    if blki == 0 and ffl == 0:
                        wait("vector", mask_ready_ms[s - 1])
                mcol = S[:, SO["mact"] + fc:SO["mact"] + fc + 1]
                ms_mk = op("vector", lambda v, ffl=ffl, mcol=mcol: v.tensor_scalar(
                    out=hblk(ffl), in0=hblk(ffl), scalar1=mcol, scalar2=None,
                    op0=ALU.mult), inc=True)
                relu_fin[fc] = ms_mk
            for ffl in range(FBLK):
                fc = blki * FBLK + ffl
                w = []
                if ffl == 0:
                    if last_xnew_mm["ms"] is not None:
                        w.append(last_xnew_mm["ms"])
                    if ck_read_ms["ms"] is not None:
                        w.append(ck_read_ms["ms"])
                        ck_read_ms["ms"] = None
                dma("sync", Abf(WP_OFF + ffl * 1024, 0, 128, 1024),
                    wp_bf[fc * 128:(fc + 1) * 128, :], waits=w)
            g_wp = qgate("sync")
            for cc in range(NC):
                for tg in range(TG):
                    bank = rbank()
                    bank_wait(bank)
                    if cc == 0 and tg == 0:
                        wait("tensor", g_wp)
                    ms_mm = None
                    for ffl in range(FBLK):
                        if cc == 0 and tg == 0:
                            wait("tensor", relu_fin[blki * FBLK + ffl])
                        def f(t, ffl=ffl, cc=cc, tg=tg, bank=bank):
                            return t.matmul(
                                bank[:, 0:512],
                                Abf(WP_OFF + ffl * 1024 + cc * 128, 0, 128, 128),
                                hblk(ffl)[:, tg * 512:(tg + 1) * 512],
                                start=(ffl == 0), stop=(ffl == FBLK - 1))
                        if ffl == FBLK - 1:
                            ms_mm = op("tensor", f, inc=True)
                        else:
                            op("tensor", f)
                    last_xnew_mm["ms"] = ms_mm
                    wait("vector", ms_mm)
                    if blki == 0:
                        def ac(v, cc=cc, tg=tg, bank=bank):
                            return v.tensor_copy(
                                out=xacc(cc)[:, tg * 512:(tg + 1) * 512],
                                in_=bank[:, 0:512])
                    else:
                        def ac(v, cc=cc, tg=tg, bank=bank):
                            return v.tensor_tensor(
                                out=xacc(cc)[:, tg * 512:(tg + 1) * 512],
                                in0=xacc(cc)[:, tg * 512:(tg + 1) * 512],
                                in1=bank[:, 0:512], op=ALU.add)
                    ms_a = op("vector", ac, inc=True)
                    bank_set(bank, ms_a)
        for nxt in tick_iter:
            nxt()

    def emit_pool_ck(sm):
        vdrain()
        pl_last = None
        for cc in range(NC):
            op("vector", lambda v, cc=cc: v.tensor_reduce(
                out=S[:, SO["pool"] + cc:SO["pool"] + cc + 1], in_=xacc(cc),
                axis=AX, op=ALU.add))
            vdrain()
            pl_last = op("vector", lambda v, cc=cc: v.tensor_scalar(
                out=S[:, SO["pool"] + cc:SO["pool"] + cc + 1],
                in0=S[:, SO["pool"] + cc:SO["pool"] + cc + 1],
                scalar1=1.0 / T, scalar2=None, op0=ALU.mult), inc=True)
        wait("tensor", ms_wc1)
        wait("tensor", pl_last)
        for d2c in range(2):
            bank_wait(ps6)
            ms_t = None
            for cc in range(NC):
                def mmf(t, d2c=d2c, cc=cc):
                    return t.matmul(
                        ps6[:, 0:1],
                        wc1_s[:, cc * 256 + d2c * 128:cc * 256 + (d2c + 1) * 128],
                        S[:, SO["pool"] + cc:SO["pool"] + cc + 1],
                        start=(cc == 0), stop=(cc == NC - 1))
                if cc < NC - 1:
                    op("tensor", mmf)
                else:
                    ms_t = op("tensor", mmf, inc=True)
            wait("vector", ms_t)
            ms_y1 = op("vector", lambda v, d2c=d2c: v.tensor_scalar(
                out=S[:, SO["y1"] + d2c:SO["y1"] + d2c + 1], in0=ps6[:, 0:1],
                scalar1=0.0, scalar2=None, op0=ALU.max), inc=True)
            bank_set(ps6, ms_y1)
        vdrain()
        y1b = S.ap().bitcast(BF16)[0:128, 2 * SO["y1b"]:2 * SO["y1b"] + 2]
        ms_y1b = op("vector", lambda v, y1b=y1b: v.tensor_copy(
            out=y1b, in_=S[:, SO["y1"]:SO["y1"] + 2]), inc=True)
        bank_wait(ps4)
        ck_done = None
        prev_g = last_h_mm["ms"]
        for g in range(3):
            dma("sync",
                Abf(WM_OFF, 0, 128, 8192).rearrange("p (d q) -> p d q", d=2),
                col_block(wc2_bf, g * 4096, 4096, 2, FF),
                waits=[prev_g])
            ms_w = qgate("sync")
            wait("tensor", ms_w)
            if g == 0:
                wait("tensor", ms_y1b)
            for fi in range(32):
                fc = g * 32 + fi
                for d2c in range(2):
                    def mmf(t, fi=fi, d2c=d2c, fc=fc):
                        return t.matmul(
                            ps4[:, fc:fc + 1],
                            Abf(WM_OFF + d2c * 4096 + fi * 128, 0, 128, 128),
                            S.ap().bitcast(BF16)[0:128,
                                                 2 * SO["y1b"] + d2c:
                                                 2 * SO["y1b"] + d2c + 1],
                            start=(d2c == 0), stop=(d2c == 1))
                    last = (fi == 31 and d2c == 1)
                    if last:
                        prev_g = op("tensor", mmf, inc=True)
                        if fc == FC - 1:
                            ck_done = prev_g
                    else:
                        op("tensor", mmf)
        wait("vector", ck_done)
        ms_ck = op("vector", lambda v: v.tensor_copy(
            out=S[:, SO["ck"]:SO["ck"] + 96], in_=ps4[:, 0:96]), inc=True)
        bank_set(ps4, ms_ck)
        ck_read_ms["ms"] = ck_done
        last_h_mm["ms"] = ck_done
        if "d_sc" in dbg:
            dma("sync", dbg["d_sc"][sm], S[:, SO["ck"]:SO["ck"] + 96],
                waits=[ms_ck])
        return ms_ck

    rhs0 = [m for m in x1_cast_ms]
    emit_step(0, rhs0)
    ms_xa0 = ("v", cnt["v"])
    ms_ck0 = emit_pool_ck(0)
    rhs1 = []
    for cc in range(NC):
        dma("sync", xn1_spill[cc * 128:(cc + 1) * 128, :], xacc(cc),
            waits=[ms_xa0])
        wait("scalar", ms_xa0)
        ms_c = op("scalar", lambda s_, cc=cc: s_.activation(
            out=xcur(cc), in_=xacc(cc), func=ACT.Identity), inc=True)
        rhs1.append(ms_c)
        if "d_xn" in dbg:
            dma("sync", dbg["d_xn"][0, cc * 128:(cc + 1) * 128, :], xacc(cc),
                waits=[ms_xa0])
    g_sp1 = qgate("sync")
    bis_prepass(0, ms_ck0)
    wait("vector", g_sp1)
    for m in rhs1:
        wait("vector", m)
    emit_step(1, rhs1)
    bis_nov_finish(0)
    ms_xa1 = ("v", cnt["v"])
    ms_ck1 = emit_pool_ck(1)
    rhs2 = []
    for cc in range(NC):
        xre = Af32(ROTF, 0, 128, 2048)
        dma("sync", xre, xn1_spill[cc * 128:(cc + 1) * 128, :],
            waits=[m for m in readers.pop(("rotf",), [])])
        ms_re = qgate("sync")
        wait("vector", ms_re)
        if cc == 0:
            wait("vector", nov_done_ms[0])
        d1 = S[:, SO["drep"]:SO["drep"] + 1]
        vdrain()
        op("vector", lambda v, xre=xre, cc=cc: v.tensor_tensor(
            out=xre, in0=xre, in1=xacc(cc), op=ALU.subtract))
        vdrain()
        op("vector", lambda v, xre=xre, d1=d1: v.tensor_scalar(
            out=xre, in0=xre, scalar1=d1, scalar2=None, op0=ALU.mult))
        vdrain()
        ms_bl = op("vector", lambda v, xre=xre, cc=cc: v.tensor_tensor(
            out=xacc(cc), in0=xacc(cc), in1=xre, op=ALU.add), inc=True)
        note_read(("rotf",), ms_bl)
        dma("sync", xo2_spill[cc * 128:(cc + 1) * 128, :], xacc(cc),
            waits=[ms_bl])
        wait("scalar", ms_bl)
        ms_c = op("scalar", lambda s_, cc=cc: s_.activation(
            out=xcur(cc), in_=xacc(cc), func=ACT.Identity), inc=True)
        rhs2.append(ms_c)
        if "d_xn" in dbg:
            dma("sync", dbg["d_xn"][1, cc * 128:(cc + 1) * 128, :], xacc(cc),
                waits=[ms_bl])
    g_sp2 = qgate("sync")
    bis_prepass(1, ms_ck1)
    wait("vector", g_sp2)
    for m in rhs2:
        wait("vector", m)
    emit_step(2, rhs2)
    bis_nov_finish(1)
    fin_ms = [None] * NC
    for cc in range(NC):
        xre = Af32(ROTF, 0, 128, 2048)
        ms_xa = ("v", cnt["v"])
        dma("sync", xre, xo2_spill[cc * 128:(cc + 1) * 128, :],
            waits=[m for m in readers.pop(("rotf",), [])])
        ms_re = qgate("sync")
        wait("vector", ms_re)
        if cc == 0:
            wait("vector", nov_done_ms[1])
        d2v = S[:, SO["drep"] + 1:SO["drep"] + 2]
        vdrain()
        op("vector", lambda v, xre=xre, cc=cc: v.tensor_tensor(
            out=xre, in0=xre, in1=xacc(cc), op=ALU.subtract))
        vdrain()
        op("vector", lambda v, xre=xre, d2v=d2v: v.tensor_scalar(
            out=xre, in0=xre, scalar1=d2v, scalar2=None, op0=ALU.mult))
        vdrain()
        ms_f = op("vector", lambda v, xre=xre, cc=cc: v.tensor_tensor(
            out=xacc(cc), in0=xacc(cc), in1=xre, op=ALU.add), inc=True)
        note_read(("rotf",), ms_f)
        fin_ms[cc] = ms_f
        if "d_xn" in dbg:
            dma("sync", dbg["d_xn"][2, cc * 128:(cc + 1) * 128, :], xacc(cc),
                waits=[ms_f])
    if "d_msk" in dbg:
        dma("sync", dbg["d_msk"][0], S[:, SO["mact"]:SO["mact"] + 96],
            waits=[mask_ready_ms[1]])
        dma("sync", dbg["d_msk"][1], S[:, SO["hist"]:SO["hist"] + 96])
        dma("sync", dbg["d_msk"][2], S[:, SO["mnew"]:SO["mnew"] + 96])
        dma("sync", dbg["d_msk"][3], S[:, SO["drep"]:SO["drep"] + 2].broadcast_to(
            [128, 96]) if False else S[:, SO["cmb"]:SO["cmb"] + 96])

    if stage == "ffn":
        outs = [dma("sync", out_d[:, :], x1_spill[:, :], waits=[fin_ms[-1]])]
        return finish(outs)

    lastT_ffn = ("t", cnt["t"])
    g_ffn_dg = qgate("gpsimd")

    def fin_src(m, ybuf):
        xre = Af32(XREROT[m % 2], 0, 128, 2048)
        dma("sync", xre, x1_spill[m * 128:(m + 1) * 128, :],
            waits=[m2 for m2 in readers.pop(("xrerot", m % 2), [])]
            + ([lastT_ffn, g_ffn_dg] if m < 2 else []))
        ms_x = qgate("sync")
        wait("vector", ms_x)
        if m == 0:
            wait("vector", lastT_ffn)
        wait("vector", fin_ms[m])
        vdrain()
        ms_y = op("vector", lambda v, xre=xre, m=m, ybuf=ybuf: v.tensor_tensor(
            out=ybuf, in0=xacc(m), in1=xre, op=ALU.add), inc=True)
        note_read(("xrerot", m % 2), ms_y)
        return ms_y

    out_ms, _ = emit_ln(fin_src, z_spill, out_d, None, [],
                        ybuf_fn=lambda m: xacc(m))
    return finish(out_ms)


def _prep_weights(ins):
    rel = np.asarray(ins["L0_attn_rel_emb"], np.float32)
    pe = np.arange(4096)
    src = np.clip(3070 - pe, 0, 2046)
    rextT = np.ascontiguousarray(rel[src].T)
    return {
        "wq": np.ascontiguousarray(ins["L0_attn_Wq"], dtype=np.float32),
        "wk": np.ascontiguousarray(ins["L0_attn_Wk"], dtype=np.float32),
        "wv": np.ascontiguousarray(ins["L0_attn_Wv"], dtype=np.float32),
        "wo": np.ascontiguousarray(ins["L0_attn_Wo"], dtype=np.float32),
        "bq": np.ascontiguousarray(
            np.asarray(ins["L0_attn_bq"], np.float32).reshape(8, 128).T),
        "bk": np.ascontiguousarray(
            np.asarray(ins["L0_attn_bk"], np.float32).reshape(8, 128).T),
        "bv": np.ascontiguousarray(
            np.asarray(ins["L0_attn_bv"], np.float32).reshape(8, 128).T),
        "bo": np.ascontiguousarray(
            np.asarray(ins["L0_attn_bo"], np.float32).reshape(8, 128).T),
        "rextT": rextT,
        "wmT": np.ascontiguousarray(np.asarray(ins["L0_ffn_Wm"], np.float32).T),
        "wpT": np.ascontiguousarray(np.asarray(ins["L0_ffn_Wp"], np.float32).T),
        "wc1T": np.ascontiguousarray(np.asarray(ins["L0_ffn_Wc1"], np.float32).T),
        "wc2T": np.ascontiguousarray(np.asarray(ins["L0_ffn_Wc2"], np.float32).T),
        "onesf": np.ones((128, 128), np.float32),
        "identb": np.eye(128, dtype=ml_dtypes.bfloat16),
    }


def kernel(**inputs):
    x = np.asarray(inputs["L0_attn_x"], np.float32)
    w = _prep_weights(inputs)
    nc, _ = build_nc("full")
    in_maps = []
    for b in range(8):
        m = dict(w)
        m["xT"] = np.ascontiguousarray(x[b].T)
        in_maps.append(m)
    res = run_bass_kernel_spmd(nc, in_maps, core_ids=list(range(8)))
    out = np.stack([np.ascontiguousarray(res.results[b]["outT"].T)
                    for b in range(8)])
    return out.astype(np.float32)


# version 17
# speedup vs baseline: 1.4894x; 1.4894x over previous
import numpy as np
import concourse.bass as bass
import concourse.mybir as mybir
from concourse.ap import AP
from concourse.bass_utils import run_bass_kernel_spmd
import ml_dtypes
from contextlib import ExitStack

F32 = mybir.dt.float32
BF16 = mybir.dt.bfloat16
AX = mybir.AxisListType.X
ALU = mybir.AluOpType
ACT = mybir.ActivationFunctionType

D = 1024
T = 2048
H = 16
NC = 8
TG = 4
QT = 16
KT = 16
FF = 12288
FC = 96
FBLK = 8
NBLK = FC // FBLK
ITERK = 4096
RNOV = 0.7
EPS_LN = 1e-5
SW = 2432
SLOT = 128 * (SW + 1) + 2560
NIT = 16
NPROBE = 7

ARENA_COLS = 94208
R0 = 0
R1 = 16384
R2 = 32768
R3 = 49152
R4 = 65536
R5 = 81920
SMALL_COLS = 2560


def build_nc(stage="full"):
    nc = bass.Bass("TRN2", target_bir_lowering=False)

    xT_d = nc.dram_tensor("xT", [D, T], F32, kind="ExternalInput")
    wq_d = nc.dram_tensor("wq", [D, D], F32, kind="ExternalInput")
    wk_d = nc.dram_tensor("wk", [D, D], F32, kind="ExternalInput")
    wv_d = nc.dram_tensor("wv", [D, D], F32, kind="ExternalInput")
    wo_d = nc.dram_tensor("wo", [D, D], F32, kind="ExternalInput")
    bq_d = nc.dram_tensor("bq", [128, 8], F32, kind="ExternalInput")
    bk_d = nc.dram_tensor("bk", [128, 8], F32, kind="ExternalInput")
    bv_d = nc.dram_tensor("bv", [128, 8], F32, kind="ExternalInput")
    bo_d = nc.dram_tensor("bo", [128, 8], F32, kind="ExternalInput")
    rextT_d = nc.dram_tensor("rextT", [64, 4096], F32, kind="ExternalInput")
    wmT_d = nc.dram_tensor("wmT", [D, FF], F32, kind="ExternalInput")
    wpT_d = nc.dram_tensor("wpT", [FF, D], F32, kind="ExternalInput")
    wc1T_d = nc.dram_tensor("wc1T", [D, 256], F32, kind="ExternalInput")
    wc2T_d = nc.dram_tensor("wc2T", [256, FF], F32, kind="ExternalInput")
    onesf_d = nc.dram_tensor("onesf", [128, 128], F32, kind="ExternalInput")
    identb_d = nc.dram_tensor("identb", [128, 128], BF16, kind="ExternalInput")
    out_d = nc.dram_tensor("outT", [D, T], F32, kind="ExternalOutput")

    skew_d = nc.dram_tensor("skew", [2 * QT * SLOT], BF16, kind="Internal")
    wq_bf = nc.dram_tensor("wq_bf", [D, D], BF16, kind="Internal")
    wk_bf = nc.dram_tensor("wk_bf", [D, D], BF16, kind="Internal")
    wv_bf = nc.dram_tensor("wv_bf", [D, D], BF16, kind="Internal")
    wo_bf = nc.dram_tensor("wo_bf", [D, D], BF16, kind="Internal")
    wm_bf = nc.dram_tensor("wm_bf", [D, FF], BF16, kind="Internal")
    wp_bf = nc.dram_tensor("wp_bf", [FF, D], BF16, kind="Internal")
    wc2_bf = nc.dram_tensor("wc2_bf", [256, FF], BF16, kind="Internal")
    y_spill = nc.dram_tensor("y_spill", [D, T], F32, kind="Internal")
    x1_spill = nc.dram_tensor("x1_spill", [D, T], F32, kind="Internal")
    xn1_spill = nc.dram_tensor("xn1_spill", [D, T], F32, kind="Internal")
    xo2_spill = nc.dram_tensor("xo2_spill", [D, T], F32, kind="Internal")
    z_spill = nc.dram_tensor("z_spill", [D, T], F32, kind="Internal")
    bounce_d = nc.dram_tensor("bounce", [1, 8], F32, kind="Internal")
    nov_in = [nc.dram_tensor(f"nov_in{i}", [1, 1], F32, kind="Internal")
              for i in range(2)]
    nov_out = [nc.dram_tensor(f"nov_out{i}", [1, 1], F32, kind="Internal",
                              addr_space="Shared") for i in range(2)]

    dbg = {}
    if stage in ("qkv", "attn", "x1", "ffn"):
        dbg["d_qt"] = nc.dram_tensor("d_qt", [D, T], BF16, kind="ExternalOutput")
        dbg["d_kt"] = nc.dram_tensor("d_kt", [D, T], BF16, kind="ExternalOutput")
        dbg["d_vt"] = nc.dram_tensor("d_vt", [D, T], BF16, kind="ExternalOutput")
    if stage in ("attn", "x1", "ffn"):
        dbg["d_ctx"] = nc.dram_tensor("d_ctx", [D, T], BF16, kind="ExternalOutput")
    if stage == "attn":
        dbg["d_sc0"] = nc.dram_tensor("d_sc0", [128, 2048], BF16, kind="ExternalOutput")
        dbg["d_band0"] = nc.dram_tensor("d_band0", [128, 2048], BF16, kind="ExternalOutput")
        dbg["d_attn0"] = nc.dram_tensor("d_attn0", [128, 2048], BF16, kind="ExternalOutput")
        dbg["d_bq0"] = nc.dram_tensor("d_bq0", [128, 2176], BF16, kind="ExternalOutput")
        dbg["d_negc"] = nc.dram_tensor("d_negc", [128, 16], F32, kind="ExternalOutput")
        dbg["d_srow"] = nc.dram_tensor("d_srow", [128, 1], F32, kind="ExternalOutput")
        dbg["d_sm"] = nc.dram_tensor("d_sm", [2, 64], F32, kind="ExternalOutput")
    if stage in ("x1", "ffn"):
        dbg["d_x1"] = nc.dram_tensor("d_x1", [D, T], F32, kind="ExternalOutput")
    if stage == "x1":
        dbg["d_y"] = nc.dram_tensor("d_y", [D, T], F32, kind="ExternalOutput")
        dbg["d_stat"] = nc.dram_tensor("d_stat", [2, 1, 2048], F32, kind="ExternalOutput")
    if stage == "ffn":
        dbg["d_xn"] = nc.dram_tensor("d_xn", [3, D, T], F32, kind="ExternalOutput")
        dbg["d_msk"] = nc.dram_tensor("d_msk", [4, 128, 96], F32,
                                      kind="ExternalOutput")
        dbg["d_sc"] = nc.dram_tensor("d_sc", [2, 128, 96], F32,
                                     kind="ExternalOutput")

    es = ExitStack()
    A = es.enter_context(nc.sbuf_tensor("arena", [128, ARENA_COLS], BF16))
    S = es.enter_context(nc.sbuf_tensor("smalls", [128, SMALL_COLS], F32))
    rextT_s = es.enter_context(nc.sbuf_tensor("rextT_s", [128, 4096], BF16))
    onesf_s = es.enter_context(nc.sbuf_tensor("onesf_s", [128, 128], F32))
    identb_s = es.enter_context(nc.sbuf_tensor("identb_s", [128, 128], BF16))
    ps0 = es.enter_context(nc.psum_tensor("ps0", [128, 512], F32))
    ps1 = es.enter_context(nc.psum_tensor("ps1", [128, 512], F32))
    pstrA = es.enter_context(nc.psum_tensor("pstrA", [128, 1024], BF16))
    pstrB = es.enter_context(nc.psum_tensor("pstrB", [128, 1024], BF16))
    ps4 = es.enter_context(nc.psum_tensor("ps4", [128, 512], F32))
    ps5 = es.enter_context(nc.psum_tensor("ps5", [128, 512], F32))
    ps6 = es.enter_context(nc.psum_tensor("ps6", [128, 512], F32))
    ps7 = es.enter_context(nc.psum_tensor("ps7", [128, 512], F32))
    sem = {}
    for name in ["dg", "ds", "t", "v", "s", "cc", "tb", "vb", "ccb", "gds", "gdg"]:
        sem[name] = es.enter_context(nc.semaphore("sem_" + name))
    blk = es.enter_context(nc.Block())

    q = {"tensor": [], "vector": [], "scalar": [], "sync": [], "gpsimd": []}
    cnt = {k: 0 for k in sem}
    ENG_SEM = {"tensor": "t", "vector": "v", "scalar": "s"}

    def emitq(e, f):
        q[e].append(f)

    QKEY = {"sync": "ds", "gpsimd": "dg"}

    def wait(e, ms):
        if ms is None:
            return
        key, val = ms
        if key in ("ds", "dg"):
            assert QKEY.get(e) == key, (e, ms)
            val = cnt[key]
        emitq(e, lambda eng, key=key, val=val: eng.wait_ge(sem[key], val))

    def op(e, fn, inc=False, sem_key=None):
        key = sem_key or ENG_SEM[e]
        if inc:
            cnt[key] += 1
            tgt = cnt[key]
            emitq(e, lambda eng, fn=fn, key=key: fn(eng).then_inc(sem[key], 1))
            return (key, tgt)
        emitq(e, fn)
        return None

    def dma(queue, out, in_, waits=(), sem_key=None):
        key = sem_key or QKEY[queue]
        for w in waits:
            if w is None:
                continue
            wk, wv = w
            if wk in ("ds", "dg"):
                if wk == QKEY[queue]:
                    tot = cnt[wk]
                    emitq(queue, lambda eng, wk=wk, tot=tot: eng.wait_ge(
                        sem[wk], tot))
                else:
                    raise AssertionError(f"cross-queue raw wait {w} on {queue}")
            else:
                wait(queue, w)
        cnt[key] += 16
        tgt = cnt[key]
        emitq(queue, lambda eng, out=out, in_=in_, key=key:
              eng.dma_start(out=out, in_=in_).then_inc(sem[key], 16))
        return (key, tgt)

    def qgate(queue):
        qk = QKEY[queue]
        gk = "gds" if queue == "sync" else "gdg"
        tot = cnt[qk]
        emitq(queue, lambda eng, qk=qk, tot=tot: eng.wait_ge(sem[qk], tot))
        cnt[gk] += 1
        tgt = cnt[gk]
        emitq(queue, lambda eng, gk=gk: eng.sem_inc(sem[gk], 1))
        return (gk, tgt)

    readers = {}

    def note_read(bufkey, ms):
        if ms is not None:
            readers.setdefault(bufkey, []).append(ms)

    def claim(e, bufkey):
        engs = e if isinstance(e, (list, tuple)) else [e]
        for ms in readers.pop(bufkey, []):
            for ee in engs:
                wait(ee, ms)

    def vdrain():
        emitq("vector", lambda eng: eng.drain())

    bank_free = {}

    def bank_wait(bank):
        if id(bank) in bank_free:
            wait("tensor", bank_free[id(bank)])

    def bank_set(bank, ms):
        bank_free[id(bank)] = ms

    def Abf(off, p0, np_, cols):
        return A[p0:p0 + np_, off:off + cols]

    def Af32(off, p0, np_, cols):
        assert off % 2 == 0
        return A.ap().bitcast(F32)[p0:p0 + np_, off // 2: off // 2 + cols]

    SO = {}
    _so = [0]
    for nm, n in [("bq", 8), ("bk8", 8), ("bv", 8), ("bo", 8), ("negc", 16),
                  ("srow", 1), ("inv", 1), ("tmp1", 1), ("mq2", 16), ("mk2", 16),
                  ("mr2", 1), ("mrrep", 1), ("hsplit", 2), ("mq2p", 8), ("mk2p", 8), ("tmp2", 1), ("krow", 16), ("crow", 16), ("negR", 16),
                  ("ck", 96), ("mact", 96), ("hist", 96), ("mnew", 96),
                  ("cmb", 96), ("cmpb", 96), ("lo", 1), ("hi", 1), ("tt", 1),
                  ("cntp", 7), ("w8", 1), ("tj", 1), ("cge", 7), ("rr", 1), ("done", 2), ("dnew", 1), ("drep", 2),
                  ("nbp", 1), ("nbs", 1), ("novs", 1), ("pool", 8), ("y1", 2),
                  ("y1b", 1), ("one_m_d", 1)]:
        SO[nm] = _so[0]
        _so[0] += n
    assert _so[0] <= SMALL_COLS

    def Sv(name, np_=128, cols=1, p0=0):
        o = SO[name]
        return S[p0:p0 + np_, o:o + cols]

    def col_block(wd, col0, width, nchunk, row_len):
        return AP(wd, col0, [[row_len, 128], [128 * row_len, nchunk], [1, width]])

    dma("sync", onesf_s[:], onesf_d[:])
    dma("sync", identb_s[:], identb_d[:])
    dma("gpsimd", rextT_s[0:64, :], rextT_d[:])
    dma("gpsimd", rextT_s[64:128, :], rextT_d[:])
    for cc in range(NC):
        dma("gpsimd", Abf(R0 + cc * 2048, 0, 128, 2048),
            xT_d[cc * 128:(cc + 1) * 128, :])
    g_x = qgate("gpsimd")
    for srcd, dstd in [(wq_d, wq_bf), (wk_d, wk_bf), (wv_d, wv_bf),
                       (wo_d, wo_bf)]:
        dma("gpsimd", dstd[:, :], srcd[:, :])
    g_cast_qkv = qgate("gpsimd")
    for srcd, dstd in [(wmT_d, wm_bf), (wpT_d, wp_bf), (wc2T_d, wc2_bf)]:
        dma("gpsimd", dstd[:, :], srcd[:, :])
    g_cast_ffn = qgate("gpsimd")

    ms_bias = {}
    for nm, dd in [("bq", bq_d), ("bk8", bk_d), ("bv", bv_d), ("bo", bo_d)]:
        dma("sync", Sv(nm, 128, 8), dd[:, :])
    g_const = qgate("sync")
    for nm in ("bq", "bk8", "bv", "bo"):
        ms_bias[nm] = g_const
    wait("vector", g_const)
    ms_bias["bk8"] = op("vector", lambda v: v.tensor_scalar(
        out=Sv("bk8", 128, 8), in0=Sv("bk8", 128, 8), scalar1=0.125, scalar2=None,
        op0=ALU.mult), inc=True)

    ring = [ps0, ps1, ps4, ps5]
    _ri = [0]

    def rbank():
        b = ring[_ri[0] % 4]
        _ri[0] += 1
        return b

    RO_wtile = [R4 + 0, R4 + 1024]
    RO_bq = [R4 + 2048 + i * 2176 for i in range(4)]
    RO_band = [R4 + 10752, R4 + 12800]
    RO_attn = [R5 + 0, R5 + 2048]
    RO_attnT = [R5 + 4096, R5 + 6144]
    RO_sc = [R5 + 8192, R5 + 10240]
    RO_sq = R5 + 8192

    def proj2(wd, bias_name, scale, dstR, vmode=False):
        wbf = {id(wq_d): wq_bf, id(wk_d): wk_bf, id(wv_d): wv_bf}[id(wd)]
        for m in range(NC):
            claim("sync", ("wtile", m % 2))
            wt = Abf(RO_wtile[m % 2], 0, 128, 1024)
            dma("sync", wt.rearrange("p (c q) -> p c q", c=8),
                col_block(wbf, m * 128, 128, 8, D),
                waits=[g_cast_qkv] if m == 0 else ())
            ms_w = qgate("sync")
            ms_e = None
            for tg in range(TG):
                bank = rbank()
                bank_wait(bank)
                if tg == 0:
                    wait("tensor", ms_w)
                    if wd is wq_d:
                        wait("tensor", g_x)
                def mk(cc, tg=tg, bank=bank, wt=wt):
                    def f(t, cc=cc, tg=tg, bank=bank, wt=wt):
                        return t.matmul(
                            bank[:, 0:512], wt[:, cc * 128:(cc + 1) * 128],
                            Abf(R0 + cc * 2048 + tg * 512, 0, 128, 512),
                            start=(cc == 0), stop=(cc == NC - 1))
                    return f
                for cc in range(NC - 1):
                    op("tensor", mk(cc))
                ms_mm = op("tensor", mk(NC - 1), inc=True)
                note_read(("wtile", m % 2), ms_mm)
                bias = S[:, SO[bias_name] + m: SO[bias_name] + m + 1]
                wait("scalar", ms_bias[bias_name])
                wait("scalar", ms_mm)
                if vmode:
                    claim("scalar", ("vstage", m % 2, tg))
                    outap = Abf(RO_attn[m % 2] + tg * 512, 0, 128, 512)
                else:
                    outap = Abf(dstR + m * 2048 + tg * 512, 0, 128, 512)
                def ev(s_, outap=outap, bank=bank, bias=bias, scale=scale):
                    return s_.activation(out=outap, in_=bank[:, 0:512],
                                         func=ACT.Identity, bias=bias, scale=scale)
                ms_e = op("scalar", ev, inc=True)
                bank_set(bank, ms_e)
            if vmode:
                wait("tensor", g_const)
                for half in range(2):
                    ptile = [pstrA, pstrB][half]
                    bank_wait(ptile)
                    wait("tensor", ms_e)
                    ms_tt = None
                    for k2 in range(8):
                        kt2 = half * 8 + k2
                        def tr(t, k2=k2, kt2=kt2, m=m, ptile=ptile):
                            return t.transpose(
                                ptile[:, k2 * 128:(k2 + 1) * 128],
                                Abf(RO_attn[m % 2] + kt2 * 128, 0, 128, 128),
                                identb_s[:])
                        if k2 < 7:
                            op("tensor", tr)
                        else:
                            ms_tt = op("tensor", tr, inc=True)
                    for tgx in (0, 1, 2, 3):
                        note_read(("vstage", m % 2, tgx), ms_tt)
                    wait("scalar", ms_tt)
                    def evt(s_, m=m, half=half, ptile=ptile):
                        return s_.activation(
                            out=Abf(R3 + m * 2048 + half * 1024, 0, 128, 1024),
                            in_=ptile[:, 0:1024], func=ACT.Identity)
                    ms_et = op("scalar", evt, inc=True)
                    bank_set(ptile, ms_et)

    proj2(wq_d, "bq", 1.0, R1)
    proj2(wk_d, "bk8", 0.125, R2)
    proj2(wv_d, "bv", 1.0, None, vmode=True)
    ms_proj_done = ("s", cnt["s"])

    sqf = Af32(RO_sq, 0, 128, 2048)
    wait("vector", g_x)
    wait("vector", ms_proj_done)
    op("vector", lambda v: v.memset(Sv("hsplit", 128, 2), 0.0))
    vdrain()
    op("vector", lambda v: v.memset(Sv("hsplit", 64, 1, 0), 1.0))
    ms_hs = op("vector", lambda v: v.memset(
        S[64:128, SO["hsplit"] + 1:SO["hsplit"] + 2], 1.0), inc=True)
    vdrain()

    first = [True]
    for half in range(2):
        sl = rextT_s[0:64, half * 2048:(half + 1) * 2048]
        sqv = Af32(RO_sq, 0, 64, 2048)
        ms_sq = op("vector", lambda v, sl=sl, sqv=sqv: v.tensor_tensor(
            out=sqv, in0=sl, in1=sl, op=ALU.mult), inc=True)
        for tg in range(TG):
            bank_wait(ps6)
            wait("tensor", ms_sq)
            wait("tensor", g_const)
            def mmf(t, tg=tg, sqv=sqv):
                return t.matmul(ps6[0:1, 0:512], onesf_s[0:64, 0:1],
                                sqv[:, tg * 512:(tg + 1) * 512],
                                start=True, stop=True)
            ms_t = op("tensor", mmf, inc=True)
            wait("vector", ms_t)
            ms_r = op("vector", lambda v: v.tensor_reduce(
                out=Sv("tmp1", 1, 1), in_=ps6[0:1, 0:512], axis=AX,
                op=ALU.max), inc=True)
            bank_set(ps6, ms_r)
            vdrain()
            if first[0]:
                op("vector", lambda v: v.tensor_copy(out=Sv("mr2", 1, 1),
                                                     in_=Sv("tmp1", 1, 1)))
                first[0] = False
            else:
                op("vector", lambda v: v.tensor_tensor(
                    out=Sv("mr2", 1, 1), in0=Sv("mr2", 1, 1),
                    in1=Sv("tmp1", 1, 1), op=ALU.max))
            vdrain()
    for which, srcR, dst in [(0, R1, "mq2p"), (1, R2, "mk2p")]:
        for cc in range(NC):
            sl = Abf(srcR + cc * 2048, 0, 128, 2048)
            ms_sq = op("vector", lambda v, sl=sl: v.tensor_tensor(
                out=sqf, in0=sl, in1=sl, op=ALU.mult), inc=True)
            for tg in range(TG):
                bank_wait(ps6)
                wait("tensor", ms_sq)
                if cc == 0 and tg == 0:
                    wait("tensor", ms_hs)
                def mmf(t, tg=tg):
                    return t.matmul(ps6[0:2, 0:512], Sv("hsplit", 128, 2),
                                    sqf[:, tg * 512:(tg + 1) * 512],
                                    start=True, stop=True)
                ms_t = op("tensor", mmf, inc=True)
                wait("vector", ms_t)
                ms_r = op("vector", lambda v: v.tensor_reduce(
                    out=Sv("tmp2", 2, 1), in_=ps6[0:2, 0:512], axis=AX,
                    op=ALU.max), inc=True)
                bank_set(ps6, ms_r)
                vdrain()
                col = S[0:2, SO[dst] + cc:SO[dst] + cc + 1]
                if tg == 0:
                    op("vector", lambda v, col=col: v.tensor_copy(
                        out=col, in_=Sv("tmp2", 2, 1)))
                else:
                    op("vector", lambda v, col=col: v.tensor_tensor(
                        out=col, in0=col, in1=Sv("tmp2", 2, 1), op=ALU.max))
                vdrain()
    ms_vfin = op("vector", lambda v: v.engine_nop(), inc=True)
    wait("scalar", ms_vfin)
    op("scalar", lambda s_: s_.activation(out=Sv("mq2p", 2, 8),
                                          in_=Sv("mq2p", 2, 8), func=ACT.Sqrt))
    op("scalar", lambda s_: s_.activation(out=Sv("mk2p", 2, 8),
                                          in_=Sv("mk2p", 2, 8), func=ACT.Sqrt))
    ms_sq3 = op("scalar", lambda s_: s_.activation(
        out=Sv("mr2", 1, 1), in_=Sv("mr2", 1, 1), func=ACT.Sqrt), inc=True)
    bank_wait(ps7)
    wait("tensor", ms_sq3)
    ms_bmr = op("tensor", lambda t: t.matmul(
        ps7[0:2, 0:1], onesf_s[0:1, 0:2], Sv("mr2", 1, 1),
        start=True, stop=True), inc=True)
    wait("vector", ms_bmr)
    ms_mrr = op("vector", lambda v: v.tensor_copy(out=Sv("mrrep", 2, 1),
                                                  in_=ps7[0:2, 0:1]), inc=True)
    bank_set(ps7, ms_mrr)
    vdrain()
    op("vector", lambda v: v.tensor_scalar(
        out=Sv("krow", 2, 8), in0=Sv("mk2p", 2, 8), scalar1=Sv("mrrep", 2, 1),
        scalar2=None, op0=ALU.add))
    vdrain()
    op("vector", lambda v: v.tensor_tensor(
        out=Sv("crow", 2, 8), in0=Sv("mq2p", 2, 8), in1=Sv("krow", 2, 8),
        op=ALU.mult))
    vdrain()
    op("vector", lambda v: v.tensor_scalar(
        out=Sv("crow", 2, 8), in0=Sv("crow", 2, 8), scalar1=-1.0,
        scalar2=None, op0=ALU.mult))
    vdrain()
    ms_cr = op("vector", lambda v: v.engine_nop(), inc=True)
    dma("sync", bounce_d[:, :], S[1:2, SO["crow"]:SO["crow"] + 8],
        waits=[ms_cr])
    dma("sync", S[0:1, SO["mk2p"]:SO["mk2p"] + 8], bounce_d[:, :],
        waits=[("ds", 0)])
    g_cr = qgate("sync")
    wait("vector", g_cr)
    op("vector", lambda v: v.memset(Sv("negR", 1, 16), 0.0))
    vdrain()
    negR_even = AP(S, SO["negR"], [[SMALL_COLS, 1], [2, 8]])
    negR_odd = AP(S, SO["negR"] + 1, [[SMALL_COLS, 1], [2, 8]])
    op("vector", lambda v: v.tensor_copy(out=negR_even, in_=Sv("crow", 1, 8)))
    ms_nR = op("vector", lambda v: v.tensor_copy(
        out=negR_odd, in_=S[0:1, SO["mk2p"]:SO["mk2p"] + 8]), inc=True)
    vdrain()
    bank_wait(ps7)
    wait("tensor", ms_nR)
    ms_bcn = op("tensor", lambda t: t.matmul(
        ps7[:, 0:16], onesf_s[0:1, :], Sv("negR", 1, 16),
        start=True, stop=True), inc=True)
    wait("vector", ms_bcn)
    ms_negc = op("vector", lambda v: v.tensor_copy(
        out=S[:, SO["negc"]:SO["negc"] + 16], in_=ps7[:, 0:16]), inc=True)
    bank_set(ps7, ms_negc)

    def finish(extra_out_ms):
        tot_ds = cnt["ds"]
        emitq("sync", lambda eng, tot=tot_ds: eng.wait_ge(sem["ds"], tot))
        tot_dg = cnt["dg"]
        emitq("gpsimd", lambda eng, tot=tot_dg: eng.wait_ge(sem["dg"], tot))
        for e in ["tensor", "vector", "scalar", "sync", "gpsimd"]:
            fns = q[e]
            if fns:
                getattr(blk, e)(lambda eng, fns=fns: [f(eng) for f in fns])
        es.close()
        return nc, dbg

    if stage == "qkv":
        outs = []
        for cc in range(NC):
            outs.append(dma("sync", dbg["d_qt"][cc * 128:(cc + 1) * 128, :],
                            Abf(R1 + cc * 2048, 0, 128, 2048),
                            waits=[ms_proj_done]))
            outs.append(dma("sync", dbg["d_kt"][cc * 128:(cc + 1) * 128, :],
                            Abf(R2 + cc * 2048, 0, 128, 2048)))
            outs.append(dma("sync", dbg["d_vt"][cc * 128:(cc + 1) * 128, :],
                            Abf(R3 + cc * 2048, 0, 128, 2048)))
        outs.append(dma("sync", out_d[:, :], xT_d[:, :]))
        return finish(outs)

    def slot_base(h, qt):
        return (h % 2) * QT * SLOT + qt * SLOT

    wgate = {}
    rgate = {}


    def emit_bprime_qt(h, qt):
        _bprime_core(h, qt, interleave=None)

    def _bprime_core(h, qt, interleave):
        cchunk, poff = h // 2, 64 * (h % 2)
        lhs = Abf(R1 + cchunk * 2048 + qt * 128, poff, 64, 128)
        bqbuf = Abf(RO_bq[qt % 4], 0, 128, 2176)
        w0 = 1920 - qt * 128
        widths = [512, 512, 512, 512, 128]
        off = 0
        ms_e_s = ms_e_v = None
        for wi, wdt in enumerate(widths):
            bank = rbank()
            bank_wait(bank)
            if wi == 0 and h == 0 and qt == 0:
                wait("tensor", g_x)
            def mmf(t, lhs=lhs, w0=w0, off=off, wdt=wdt, bank=bank, poff=poff):
                return t.matmul(bank[:, 0:wdt], lhs,
                                rextT_s[poff:poff + 64,
                                        w0 + off:w0 + off + wdt],
                                start=True, stop=True)
            ms_t = op("tensor", mmf, inc=True)
            if interleave is not None:
                nxt = next(interleave, None)
                if nxt is not None:
                    nxt()
            ev_eng = "scalar" if wi < 2 else "vector"
            wait(ev_eng, ms_t)
            if wi == 0:
                claim(("scalar", "vector"), ("bq", qt % 4))
            if ev_eng == "scalar":
                def ev(s_, off=off, wdt=wdt, bank=bank, bqbuf=bqbuf):
                    return s_.activation(out=bqbuf[:, off:off + wdt],
                                         in_=bank[:, 0:wdt], func=ACT.Identity)
            else:
                def ev(v, off=off, wdt=wdt, bank=bank, bqbuf=bqbuf):
                    return v.tensor_copy(out=bqbuf[:, off:off + wdt],
                                         in_=bank[:, 0:wdt])
            ms_e = op(ev_eng, ev, inc=True)
            if ev_eng == "scalar":
                ms_e_s = ms_e
            else:
                ms_e_v = ms_e
            bank_set(bank, ms_e)
            off += wdt
        base = slot_base(h, qt)
        diag = AP(skew_d, base, [[SW + 1, 128], [1, 2176]])
        dma("sync", diag, bqbuf,
            waits=[m for m in (ms_e_s, ms_e_v) if m is not None]
            + [m for m in readers.pop(("skslot", h % 2, qt), [])])
        if qt % 4 == 3:
            g = qgate("sync")
            for q2 in range(qt - 3, qt + 1):
                note_read(("bq", q2 % 4), g)
            wgate[(h, qt // 4)] = g

    sc_state = {}

    def scores_part1(h, qt, bprime_next=None):
        cchunk, poff = h // 2, 64 * (h % 2)
        negc = S[:, SO["negc"] + h:SO["negc"] + h + 1]
        lhs = Abf(R1 + cchunk * 2048 + qt * 128, poff, 64, 128)
        scb = Abf(RO_sc[qt % 2], 0, 128, 2048)
        attnb = Abf(RO_attn[qt % 2], 0, 128, 2048)
        bandb = Abf(RO_band[qt % 2], 0, 128, 2048)
        if qt % 2 == 0:
            for q2 in (qt, qt + 1):
                base2 = slot_base(h, q2)
                rd = AP(skew_d, base2 + 127, [[SW, 128], [1, 2048]])
                dma("sync", Abf(RO_band[q2 % 2], 0, 128, 2048), rd,
                    waits=([wgate[(h, q2 // 4)]] if q2 % 4 == 0 else [])
                    + [m for m in readers.pop(("band", q2 % 2), [])])
            g = qgate("sync")
            rgate[(h, qt // 2)] = g
            note_read(("skslot", h % 2, qt), g)
            note_read(("skslot", h % 2, qt + 1), g)

        ms_add = None

        def qk_fill(tg):
            nonlocal ms_add
            bank = rbank()
            bank_wait(bank)
            def mmf(t, lhs=lhs, tg=tg, bank=bank, cchunk=cchunk, poff=poff):
                return t.matmul(
                    bank[:, 0:512], lhs,
                    Abf(R2 + cchunk * 2048 + tg * 512, poff, 64, 512),
                    start=True, stop=True)
            ms_t = op("tensor", mmf, inc=True)
            wait("vector", ms_t)
            if tg == 0:
                claim("vector", ("sc", qt % 2))
                wait("vector", rgate[(h, qt // 2)])
            def cl(v, tg=tg, bank=bank, scb=scb, bandb=bandb):
                return v.tensor_tensor(out=scb[:, tg * 512:(tg + 1) * 512],
                                       in0=bank[:, 0:512],
                                       in1=bandb[:, tg * 512:(tg + 1) * 512],
                                       op=ALU.add)
            ms_c = op("vector", cl, inc=True)
            bank_set(bank, ms_c)
            ms_add = ms_c

        if bprime_next is not None:
            _bprime_core(bprime_next[0], bprime_next[1],
                         iter([lambda tg=tg: qk_fill(tg) for tg in range(TG)]))
        else:
            for tg in range(TG):
                qk_fill(tg)
        note_read(("band", qt % 2), ms_add)
        if h == 0 and qt == 0:
            wait("scalar", ms_negc)
        wait("scalar", ms_add)
        claim("scalar", ("attn", qt % 2))
        ms_exp = op("scalar", lambda s_, scb=scb, attnb=attnb, negc=negc:
                    s_.activation(out=attnb, in_=scb, func=ACT.Exp, bias=negc,
                                  scale=1.0, accum_out=Sv("srow", 128, 1)),
                    inc=True)
        note_read(("sc", qt % 2), ms_exp)
        if h == 0 and qt == 0 and "d_negc" in dbg:
            dma("sync", dbg["d_negc"][:, :], S[:, SO["negc"]:SO["negc"] + 16],
                waits=[ms_exp])
            dma("sync", dbg["d_sm"][:, :], S[0:2, SO["mq2p"]:SO["mq2p"] + 64])
            dma("sync", dbg["d_srow"][:, :], Sv("srow", 128, 1))
            dma("sync", dbg["d_sc0"][:, :], scb)
            dma("sync", dbg["d_band0"][:, :], bandb)
        ms_tmp1 = op("scalar", lambda s_: s_.activation(
            out=Sv("tmp1", 128, 1), in_=Sv("srow", 128, 1),
            func=ACT.Identity, scale=1.0 + 1e-8), inc=True)
        wait("vector", ms_tmp1)
        op("vector", lambda v: v.reciprocal(out=Sv("inv", 128, 1),
                                            in_=Sv("tmp1", 128, 1)))
        vdrain()
        ms_scl = op("vector", lambda v, attnb=attnb: v.tensor_scalar(
            out=attnb, in0=attnb, scalar1=Sv("inv", 128, 1), scalar2=None,
            op0=ALU.mult), inc=True)
        sc_state[(h, qt)] = ms_scl

    def scores_part2(h, qt):
        cchunk, poff = h // 2, 64 * (h % 2)
        attnb = Abf(RO_attn[qt % 2], 0, 128, 2048)
        attnTb = Abf(RO_attnT[qt % 2], 0, 128, 2048)
        ms_scl = sc_state.pop((h, qt))
        ev_ms = []
        for half in range(2):
            ptile = [pstrA, pstrB][half]
            bank_wait(ptile)
            if half == 0:
                wait("tensor", ms_scl)
            ms_tt = None
            for k2 in range(8):
                kt2 = half * 8 + k2
                def tr(t, k2=k2, kt2=kt2, attnb=attnb, ptile=ptile):
                    return t.transpose(ptile[:, k2 * 128:(k2 + 1) * 128],
                                       attnb[:, kt2 * 128:kt2 * 128 + 128],
                                       identb_s[:])
                if k2 < 7:
                    op("tensor", tr)
                else:
                    ms_tt = op("tensor", tr, inc=True)
            if half == 1:
                note_read(("attn", qt % 2), ms_tt)
            ev_eng = "scalar" if half == 0 else "vector"
            wait(ev_eng, ms_tt)
            if half == 0:
                claim(("scalar", "vector"), ("attnT", qt % 2))
            if ev_eng == "scalar":
                def evt(s_, half=half, attnTb=attnTb, ptile=ptile):
                    return s_.activation(
                        out=attnTb[:, half * 1024:half * 1024 + 1024],
                        in_=ptile[:, 0:1024], func=ACT.Identity)
            else:
                def evt(v, half=half, attnTb=attnTb, ptile=ptile):
                    return v.tensor_copy(
                        out=attnTb[:, half * 1024:half * 1024 + 1024],
                        in_=ptile[:, 0:1024])
            ms_et = op(ev_eng, evt, inc=True)
            bank_set(ptile, ms_et)
            ev_ms.append(ms_et)
        bank = [ps6, ps7][qt % 2]
        bank_wait(bank)
        wait("tensor", ev_ms[0])
        wait("tensor", ev_ms[1])
        ms_ctx = None
        for kt2 in range(KT):
            def mmf(t, kt2=kt2, bank=bank, attnTb=attnTb, cchunk=cchunk,
                    poff=poff):
                return t.matmul(
                    bank[0:64, 0:128],
                    Abf(R3 + cchunk * 2048 + kt2 * 128 + poff, 0, 128, 64),
                    attnTb[:, kt2 * 128:kt2 * 128 + 128],
                    start=(kt2 == 0), stop=(kt2 == KT - 1))
            if kt2 < KT - 1:
                op("tensor", mmf)
            else:
                ms_ctx = op("tensor", mmf, inc=True)
        note_read(("attnT", qt % 2), ms_ctx)
        wait("vector", ms_ctx)
        def evc(v, bank=bank, qt=qt, cchunk=cchunk, poff=poff):
            return v.tensor_copy(
                out=Abf(R0 + cchunk * 2048 + qt * 128, poff, 64, 128),
                in_=bank[0:64, 0:128])
        ms_ec = op("vector", evc, inc=True)
        bank_set(bank, ms_ec)

    for qt in range(QT):
        emit_bprime_qt(0, qt)
    seq = [(h, qt) for h in range(H) for qt in range(QT)]
    prev = None
    for idx, (h, qt) in enumerate(seq):
        scores_part1(h, qt,
                     bprime_next=((h + 1, qt) if h + 1 < H else None))
        if prev is not None:
            scores_part2(*prev)
        prev = (h, qt)
    scores_part2(*prev)
    ms_ctx_all = ("v", cnt["v"])

    if stage == "attn":
        outs = []
        for cc in range(NC):
            outs.append(dma("sync", dbg["d_ctx"][cc * 128:(cc + 1) * 128, :],
                            Abf(R0 + cc * 2048, 0, 128, 2048), waits=[ms_ctx_all]))
        outs.append(dma("sync", out_d[:, :], xT_d[:, :]))
        return finish(outs)

    if stage == "x1":
        for cc in range(NC):
            dma("sync", dbg["d_ctx"][cc * 128:(cc + 1) * 128, :],
                Abf(R0 + cc * 2048, 0, 128, 2048), waits=[ms_ctx_all])
    YROT = [R4 + 0, R4 + 4096]
    XREROT = [R4 + 8192, R4 + 12288]
    SQROT = R5 + 0
    STATROW = R5 + 4096

    def stat_rows():
        return Af32(STATROW, 0, 1, 2048), Af32(STATROW + 4096, 0, 1, 2048)

    ln_done = {"ms": None}

    def emit_ln(src_fn, spill_dram, norm_out_dram, dst_bf_R, x1cast_ms_out,
                tapf=None, ybuf_fn=None):
        assert ybuf_fn is not None
        """src_fn(m, ybuf)->ms of y ready (vector ms). Builds y chunks, stats;
        if ybuf_fn given, y chunks live in SBUF (no spill/re-read); else spill
        to spill_dram and re-read. Then normalize into norm_out_dram."""
        sumr, sqr = stat_rows()
        spill_ms = []
        for m in range(NC):
            if ybuf_fn is not None:
                ybuf = ybuf_fn(m)
            else:
                ybuf = Af32(YROT[m % 2], 0, 128, 2048)
                claim(("vector", "scalar"), ("yrot", m % 2))
            ms_y = src_fn(m, ybuf)
            wait("scalar", ms_y)
            claim("scalar", ("sqrot",))
            ms_sq = op("scalar", lambda s_, ybuf=ybuf: s_.activation(
                out=Af32(SQROT, 0, 128, 2048), in_=ybuf, func=ACT.Square), inc=True)
            for which, srcf, msdep, _bx in [
                    (0, ybuf, ms_y, None),
                    (1, Af32(SQROT, 0, 128, 2048), ms_sq, None)]:
                row = sumr if which == 0 else sqr
                for tg in range(TG):
                    bankx = rbank()
                    bank_wait(bankx)
                    wait("tensor", msdep)
                    def mmf(t, srcf=srcf, tg=tg, bankx=bankx):
                        return t.matmul(bankx[0:1, 0:512], onesf_s[:, 0:1],
                                        srcf[:, tg * 512:(tg + 1) * 512],
                                        start=True, stop=True)
                    ms_t = op("tensor", mmf, inc=True)
                    if which == 0:
                        note_read(("yrot", m % 2), ms_t)
                    else:
                        note_read(("sqrot",), ms_t)
                    wait("vector", ms_t)
                    if m == 0:
                        def ac(v, row=row, tg=tg, bankx=bankx):
                            return v.tensor_copy(out=row[:, tg * 512:(tg + 1) * 512],
                                                 in_=bankx[0:1, 0:512])
                    else:
                        def ac(v, row=row, tg=tg, bankx=bankx):
                            return v.tensor_tensor(
                                out=row[:, tg * 512:(tg + 1) * 512],
                                in0=row[:, tg * 512:(tg + 1) * 512],
                                in1=bankx[0:1, 0:512], op=ALU.add)
                    ms_a = op("vector", ac, inc=True)
                    bank_set(bankx, ms_a)
                vdrain()
            if ybuf_fn is None:
                dma("sync", spill_dram[m * 128:(m + 1) * 128, :], ybuf,
                    waits=[ms_sq])
                g_sp = qgate("sync")
                note_read(("yrot", m % 2), g_sp)
                spill_ms.append(g_sp)
        vdrain()
        op("vector", lambda v: v.tensor_scalar(out=sumr, in0=sumr, scalar1=1.0 / D,
                                               scalar2=None, op0=ALU.mult))
        vdrain()
        op("vector", lambda v: v.tensor_scalar(out=sqr, in0=sqr, scalar1=1.0 / D,
                                               scalar2=None, op0=ALU.mult))
        vdrain()
        claim("vector", ("sqrot",))
        op("vector", lambda v: v.tensor_tensor(out=Af32(SQROT, 0, 1, 2048),
                                               in0=sumr, in1=sumr, op=ALU.mult))
        vdrain()
        op("vector", lambda v: v.tensor_tensor(out=sqr, in0=sqr,
                                               in1=Af32(SQROT, 0, 1, 2048),
                                               op=ALU.subtract))
        vdrain()
        ms_var = op("vector", lambda v: v.tensor_scalar(
            out=sqr, in0=sqr, scalar1=EPS_LN, scalar2=None, op0=ALU.add), inc=True)
        vdrain()
        note_read(("sqrot",), ms_var)
        wait("scalar", ms_var)
        ms_sqt = op("scalar", lambda s_: s_.activation(out=sqr, in_=sqr,
                                                       func=ACT.Sqrt), inc=True)
        wait("vector", ms_sqt)
        ms_rstd = op("vector", lambda v: v.reciprocal(out=sqr, in_=sqr), inc=True)
        vdrain()
        if "d_stat" in dbg and spill_dram.name == "y_spill":
            dma("sync", dbg["d_stat"][0], sumr, waits=[ms_rstd])
            dma("sync", dbg["d_stat"][1], sqr)
        meanb = Af32(YROT[0], 0, 128, 2048)
        rstdb = Af32(YROT[1], 0, 128, 2048)
        ms_bc = None
        for tg in range(TG):
            bank_wait(ps6)
            bank_wait(ps7)
            if tg == 0:
                wait("tensor", ms_rstd)
            ms_m = op("tensor", lambda t, tg=tg: t.matmul(
                ps6[:, 0:512], onesf_s[0:1, :],
                sumr[:, tg * 512:(tg + 1) * 512], start=True, stop=True),
                inc=True)
            ms_r2 = op("tensor", lambda t, tg=tg: t.matmul(
                ps7[:, 0:512], onesf_s[0:1, :],
                sqr[:, tg * 512:(tg + 1) * 512], start=True, stop=True),
                inc=True)
            wait("vector", ms_r2)
            op("vector", lambda v, tg=tg: v.tensor_copy(
                out=meanb[:, tg * 512:(tg + 1) * 512], in_=ps6[:, 0:512]))
            ms_bc = op("vector", lambda v, tg=tg: v.tensor_copy(
                out=rstdb[:, tg * 512:(tg + 1) * 512], in_=ps7[:, 0:512]),
                inc=True)
            bank_set(ps6, ms_bc)
            bank_set(ps7, ms_bc)
        out_ms = []
        for m in range(NC):
            if ybuf_fn is not None:
                yre = ybuf_fn(m)
                ms_re = None
            else:
                yre = Af32(YROT[m % 2], 0, 128, 2048)
                dma("sync", yre, spill_dram[m * 128:(m + 1) * 128, :],
                    waits=[m2 for m2 in readers.pop(("yrot", m % 2), [])])
                ms_re = qgate("sync")
            x1b = Af32(XREROT[m % 2], 0, 128, 2048)
            claim("vector", ("xrerot", m % 2))
            if m == 0:
                wait("vector", ms_bc)
            if ms_re is not None:
                wait("vector", ms_re)
            ms_mu = None
            def sub(v, yre=yre, x1b=x1b):
                return v.tensor_tensor(out=x1b, in0=yre, in1=meanb,
                                       op=ALU.subtract)
            op("vector", sub)
            def mul(v, x1b=x1b):
                return v.tensor_tensor(out=x1b, in0=x1b, in1=rstdb,
                                       op=ALU.mult)
            ms_mu = op("vector", mul, inc=True)
            note_read(("yrot", 0), ms_mu)
            note_read(("yrot", 1), ms_mu)
            ln_done["ms"] = ms_mu
            dma("sync", norm_out_dram[m * 128:(m + 1) * 128, :], x1b,
                waits=[ms_mu])
            if tapf is not None:
                dma("sync", tapf[m * 128:(m + 1) * 128, :], x1b)
            ms_o = qgate("sync")
            note_read(("xrerot", m % 2), ms_o)
            out_ms.append(ms_o)
            if dst_bf_R is not None:
                wait("scalar", ms_mu)
                ms_cst = op("scalar", lambda s_, x1b=x1b, m=m: s_.activation(
                    out=Abf(dst_bf_R + m * 2048, 0, 128, 2048), in_=x1b,
                    func=ACT.Identity), inc=True)
                note_read(("xrerot", m % 2), ms_cst)
                x1cast_ms_out.append(ms_cst)
        return out_ms, spill_ms

    def ybuf3(m):
        if m < 4:
            return Af32(R1 + m * 4096, 0, 128, 2048)
        return Af32(R3 + (m - 4) * 4096, 0, 128, 2048)

    def attn_src(m, ybuf):
        claim("sync", ("wotile", m % 2))
        wt = Abf(R2 + (m % 2) * 1024, 0, 128, 1024)
        dma("sync", wt.rearrange("p (c q) -> p c q", c=8),
            col_block(wo_bf, m * 128, 128, 8, D),
            waits=([ms_ctx_all] if m < 2 else []) + ([g_cast_qkv] if m == 0 else []))
        ms_w = qgate("sync")
        xre = Af32(XREROT[m % 2], 0, 128, 2048)
        dma("sync", xre, xT_d[m * 128:(m + 1) * 128, :],
            waits=[m2 for m2 in readers.pop(("xrerot", m % 2), [])]
            + ([ms_ctx_all] if m < 2 else []))
        ms_x = qgate("sync")
        ms_last = None
        for tg in range(TG):
            bank = rbank()
            bank_wait(bank)
            if tg == 0:
                wait("tensor", ms_w)
                wait("tensor", ms_ctx_all)
            def mk(dd, tg=tg, bank=bank, wt=wt):
                def f(t, dd=dd, tg=tg, bank=bank, wt=wt):
                    return t.matmul(bank[:, 0:512], wt[:, dd * 128:(dd + 1) * 128],
                                    Abf(R0 + dd * 2048 + tg * 512, 0, 128, 512),
                                    start=(dd == 0), stop=(dd == NC - 1))
                return f
            for dd in range(NC - 1):
                op("tensor", mk(dd))
            ms_mm = op("tensor", mk(NC - 1), inc=True)
            note_read(("wotile", m % 2), ms_mm)
            bias = S[:, SO["bo"] + m:SO["bo"] + m + 1]
            wait("scalar", ms_bias["bo"])
            wait("scalar", ms_mm)
            def ev(s_, tg=tg, bank=bank, ybuf=ybuf, bias=bias):
                return s_.activation(out=ybuf[:, tg * 512:(tg + 1) * 512],
                                     in_=bank[:, 0:512], func=ACT.Identity,
                                     bias=bias, scale=1.0)
            ms_last = op("scalar", ev, inc=True)
            bank_set(bank, ms_last)
        wait("vector", ms_last)
        wait("vector", ms_x)
        ms_y = op("vector", lambda v, ybuf=ybuf, xre=xre: v.tensor_tensor(
            out=ybuf, in0=ybuf, in1=xre, op=ALU.add), inc=True)
        note_read(("xrerot", m % 2), ms_y)
        return ms_y

    x1_cast_ms = []
    x1_out_ms, _ = emit_ln(attn_src, y_spill, x1_spill, R2, x1_cast_ms,
                           tapf=dbg.get("d_x1"), ybuf_fn=ybuf3)

    if stage == "x1":
        outs = [dma("sync", out_d[:, :], x1_spill[:, :],
                    waits=[x1_out_ms[-1]]),
                dma("sync", dbg["d_y"][:, :], y_spill[:, :])]
        return finish(outs)

    def xacc(cc):
        if cc < 4:
            return Af32(R0 + cc * 4096, 0, 128, 2048)
        return Af32(R1 + (cc - 4) * 4096, 0, 128, 2048)

    def xcur(cc):
        return Abf(R2 + cc * 2048, 0, 128, 2048)

    def hblk(ffl):
        return Abf(R3 + ffl * 2048, 0, 128, 2048)

    WM_OFF = R4
    WP_OFF = R4 + 8192
    ROTF = R5 + 0
    WC1_OFF = R5 + 4096

    wc1_s = Af32(WC1_OFF, 0, 128, 2048)
    dma("sync", wc1_s.rearrange("p (c q) -> p c q", c=8),
        col_block(wc1T_d, 0, 256, 8, 256),
        waits=[("t", cnt["t"]), ("v", cnt["v"])])
    ms_wc1 = qgate("sync")
    wait("sync", ("v", cnt["v"]))
    wait("sync", ("s", cnt["s"]))

    op("vector", lambda v: v.memset(S[:, SO["mact"]:SO["mact"] + 32], 1.0))
    ms_mbase = op("vector", lambda v: v.memset(
        S[:, SO["mact"] + 32:SO["mact"] + 96], 0.0), inc=True)
    vdrain()
    op("vector", lambda v: v.tensor_copy(out=S[:, SO["hist"]:SO["hist"] + 96],
                                         in_=S[:, SO["mact"]:SO["mact"] + 96]))
    vdrain()
    op("vector", lambda v: v.memset(S[:, SO["done"]:SO["done"] + 2], 0.0))
    op("vector", lambda v: v.memset(S[:, SO["drep"]:SO["drep"] + 2], 0.0))

    mask_ready_ms = {0: None, 1: None}
    nov_done_ms = {}
    bis_tensor_queue = []
    bis_gp = {}

    def bis_prepass(sm, ms_ck):
        wait("vector", ms_ck)
        op("vector", lambda v: v.memset(Sv("lo", 128, 1), -1024.0))
        op("vector", lambda v: v.memset(Sv("hi", 128, 1), 1024.0))
        ps6_free = bank_free.get(id(ps6))
        for i in range(NIT):
            vdrain()
            op("vector", lambda v: v.tensor_tensor(out=Sv("w8", 128, 1),
                                                   in0=Sv("hi", 128, 1),
                                                   in1=Sv("lo", 128, 1),
                                                   op=ALU.subtract))
            vdrain()
            op("vector", lambda v: v.tensor_scalar(out=Sv("w8", 128, 1),
                                                   in0=Sv("w8", 128, 1),
                                                   scalar1=1.0 / (NPROBE + 1.0),
                                                   scalar2=None, op0=ALU.mult))
            vdrain()
            ms_cnt = None
            for jp in range(NPROBE):
                op("vector", lambda v, jp=jp: v.tensor_scalar(
                    out=Sv("tj", 128, 1), in0=Sv("w8", 128, 1),
                    scalar1=float(jp + 1), scalar2=None, op0=ALU.mult))
                vdrain()
                op("vector", lambda v: v.tensor_tensor(
                    out=Sv("tj", 128, 1), in0=Sv("tj", 128, 1),
                    in1=Sv("lo", 128, 1), op=ALU.add))
                vdrain()
                op("vector", lambda v: v.tensor_scalar(
                    out=S[:, SO["cmpb"]:SO["cmpb"] + 96],
                    in0=S[:, SO["ck"]:SO["ck"] + 96], scalar1=Sv("tj", 128, 1),
                    scalar2=None, op0=ALU.is_ge))
                vdrain()
                red = lambda v, jp=jp: v.tensor_reduce(
                    out=S[:, SO["cntp"] + jp:SO["cntp"] + jp + 1],
                    in_=S[:, SO["cmpb"]:SO["cmpb"] + 96], axis=AX, op=ALU.add)
                if jp == NPROBE - 1:
                    ms_cnt = op("vector", red, inc=True, sem_key="vb")
                else:
                    op("vector", red)
            cnt["tb"] += 1
            tbi = cnt["tb"]
            def tclos(ms_cnt=ms_cnt, ps6_free=ps6_free):
                wait("tensor", ms_cnt)
                if ps6_free is not None:
                    wait("tensor", ps6_free)
                emitq("tensor", lambda t: t.matmul(
                    ps6[:, 0:NPROBE], onesf_s[:], Sv("cntp", 128, NPROBE),
                    start=True, stop=True).then_inc(sem["tb"], 1))
            bis_tensor_queue.append(tclos)
            wait("vector", ("tb", tbi))
            ms_ev = op("vector", lambda v: v.tensor_copy(
                out=Sv("cntp", 128, NPROBE), in_=ps6[:, 0:NPROBE]), inc=True)
            ps6_free = ms_ev
            vdrain()
            op("vector", lambda v: v.tensor_scalar(
                out=Sv("cge", 128, NPROBE), in0=Sv("cntp", 128, NPROBE),
                scalar1=float(ITERK), scalar2=None, op0=ALU.is_ge))
            vdrain()
            op("vector", lambda v: v.tensor_reduce(
                out=Sv("rr", 128, 1), in_=Sv("cge", 128, NPROBE), axis=AX,
                op=ALU.add))
            vdrain()
            op("vector", lambda v: v.tensor_tensor(
                out=Sv("rr", 128, 1), in0=Sv("rr", 128, 1), in1=Sv("w8", 128, 1),
                op=ALU.mult))
            vdrain()
            op("vector", lambda v: v.tensor_tensor(
                out=Sv("lo", 128, 1), in0=Sv("lo", 128, 1), in1=Sv("rr", 128, 1),
                op=ALU.add))
            vdrain()
            op("vector", lambda v: v.tensor_tensor(
                out=Sv("hi", 128, 1), in0=Sv("lo", 128, 1), in1=Sv("w8", 128, 1),
                op=ALU.add))
        vdrain()
        op("vector", lambda v: v.tensor_scalar(
            out=S[:, SO["mnew"]:SO["mnew"] + 96],
            in0=S[:, SO["ck"]:SO["ck"] + 96], scalar1=Sv("lo", 128, 1),
            scalar2=None, op0=ALU.is_ge))
        vdrain()
        if sm == 0:
            ms_ma = op("vector", lambda v: v.tensor_copy(
                out=S[:, SO["mact"]:SO["mact"] + 96],
                in_=S[:, SO["mnew"]:SO["mnew"] + 96]), inc=True)
        else:
            dprev = S[:, SO["drep"]:SO["drep"] + 1]
            op("vector", lambda v, dprev=dprev: v.tensor_scalar(
                out=Sv("one_m_d", 128, 1), in0=dprev, scalar1=-1.0, scalar2=1.0,
                op0=ALU.mult, op1=ALU.add))
            vdrain()
            op("vector", lambda v: v.tensor_tensor(
                out=S[:, SO["cmpb"]:SO["cmpb"] + 96],
                in0=S[:, SO["mnew"]:SO["mnew"] + 96],
                in1=S[:, SO["mact"]:SO["mact"] + 96], op=ALU.subtract))
            vdrain()
            op("vector", lambda v: v.tensor_scalar(
                out=S[:, SO["cmpb"]:SO["cmpb"] + 96],
                in0=S[:, SO["cmpb"]:SO["cmpb"] + 96],
                scalar1=Sv("one_m_d", 128, 1), scalar2=None, op0=ALU.mult))
            vdrain()
            ms_ma = op("vector", lambda v: v.tensor_tensor(
                out=S[:, SO["mact"]:SO["mact"] + 96],
                in0=S[:, SO["mact"]:SO["mact"] + 96],
                in1=S[:, SO["cmpb"]:SO["cmpb"] + 96], op=ALU.add), inc=True)
        vdrain()
        mask_ready_ms[sm] = ms_ma
        op("vector", lambda v: v.tensor_tensor(
            out=S[:, SO["cmb"]:SO["cmb"] + 96],
            in0=S[:, SO["hist"]:SO["hist"] + 96],
            in1=S[:, SO["mnew"]:SO["mnew"] + 96], op=ALU.add))
        vdrain()
        op("vector", lambda v: v.tensor_scalar(
            out=S[:, SO["cmb"]:SO["cmb"] + 96],
            in0=S[:, SO["cmb"]:SO["cmb"] + 96], scalar1=1.0, scalar2=None,
            op0=ALU.min))
        vdrain()
        op("vector", lambda v: v.tensor_tensor(
            out=S[:, SO["cmpb"]:SO["cmpb"] + 96],
            in0=S[:, SO["cmb"]:SO["cmb"] + 96],
            in1=S[:, SO["hist"]:SO["hist"] + 96], op=ALU.subtract))
        vdrain()
        ms_nb = op("vector", lambda v: v.tensor_reduce(
            out=Sv("nbp", 128, 1), in_=S[:, SO["cmpb"]:SO["cmpb"] + 96], axis=AX,
            op=ALU.add), inc=True, sem_key="vb")
        if sm == 0:
            op("vector", lambda v: v.tensor_copy(
                out=S[:, SO["hist"]:SO["hist"] + 96],
                in_=S[:, SO["cmb"]:SO["cmb"] + 96]))
        cnt["tb"] += 1
        tb_nb = cnt["tb"]
        def tclos_nb(ms_nb=ms_nb, ps6_free=ps6_free):
            wait("tensor", ms_nb)
            if ps6_free is not None:
                wait("tensor", ps6_free)
            emitq("tensor", lambda t: t.matmul(
                ps6[:, 0:1], onesf_s[:], Sv("nbp", 128, 1), start=True,
                stop=True).then_inc(sem["tb"], 1))
        bis_tensor_queue.append(tclos_nb)
        wait("vector", ("tb", tb_nb))
        ms_nbs = op("vector", lambda v: v.tensor_copy(
            out=Sv("nbs", 1, 1), in_=ps6[0:1, 0:1]), inc=True)
        bank_set(ps6, ms_nbs)
        bis_gp[sm] = ms_nbs

    def bis_nov_finish(sm):
        wait("vector", ("ccb", 16 * (sm + 1)))
        op("vector", lambda v: v.tensor_scalar(
            out=Sv("novs", 1, 1), in0=Sv("novs", 1, 1),
            scalar1=1.0 / (8.0 * ITERK), scalar2=None, op0=ALU.mult))
        vdrain()
        op("vector", lambda v: v.tensor_scalar(
            out=Sv("dnew", 1, 1), in0=Sv("novs", 1, 1), scalar1=RNOV,
            scalar2=None, op0=ALU.is_lt))
        vdrain()
        if sm == 0:
            ms_d = op("vector", lambda v: v.tensor_copy(
                out=S[0:1, SO["done"]:SO["done"] + 1], in_=Sv("dnew", 1, 1)),
                inc=True)
        else:
            ms_d = op("vector", lambda v: v.tensor_tensor(
                out=S[0:1, SO["done"] + 1:SO["done"] + 2],
                in0=S[0:1, SO["done"]:SO["done"] + 1],
                in1=Sv("dnew", 1, 1), op=ALU.max), inc=True)
        bank_wait(ps7)
        wait("tensor", ms_d)
        ms_bc = op("tensor", lambda t, sm=sm: t.matmul(
            ps7[:, 0:1], onesf_s[0:1, :],
            S[0:1, SO["done"] + sm:SO["done"] + sm + 1],
            start=True, stop=True), inc=True)
        wait("vector", ms_bc)
        ms_dr = op("vector", lambda v, sm=sm: v.tensor_copy(
            out=S[:, SO["drep"] + sm:SO["drep"] + sm + 1], in_=ps7[:, 0:1]),
            inc=True)
        vdrain()
        bank_set(ps7, ms_dr)
        nov_done_ms[sm] = ms_dr

    def bis_gpsimd(sm):
        ms_nbs = bis_gp[sm]
        ms_in = dma("gpsimd", nov_in[sm][:], Sv("nbs", 1, 1), waits=[ms_nbs])
        wait("gpsimd", ms_in)
        cnt["cc"] += 1
        tgt = cnt["cc"]
        emitq("gpsimd", lambda g, sm=sm: g.collective_compute(
            "AllReduce", ALU.add, replica_groups=[list(range(8))],
            ins=[nov_in[sm][:]], outs=[nov_out[sm][:]]).then_inc(sem["cc"], 1))
        wait("gpsimd", ("cc", tgt))
        dma("gpsimd", Sv("novs", 1, 1), nov_out[sm][:], sem_key="ccb")

    last_h_mm = {"ms": None}
    last_xnew_mm = {"ms": None}
    ck_read_ms = {"ms": None}

    def emit_step(s, rhs_ready):
        tick_iter = iter(list(bis_tensor_queue))
        bis_tensor_queue.clear()
        relu_fin = [None] * FC
        for blki in range(NBLK):
            for cc in range(NC):
                dma("sync", Abf(WM_OFF + cc * 1024, 0, 128, 1024),
                    wm_bf[cc * 128:(cc + 1) * 128,
                          blki * FBLK * 128:(blki + 1) * FBLK * 128],
                    waits=([last_h_mm["ms"]] if (cc == 0 and last_h_mm["ms"])
                           else []) + ([g_cast_ffn, ln_done["ms"]]
                                       if (s == 0 and blki == 0 and cc == 0)
                                       else []))
            g_wm = qgate("sync")
            if blki == 2:
                if s >= 1:
                    bis_gpsimd(s - 1)
            prev_xnew = last_xnew_mm["ms"]
            for ffl in range(FBLK):
                fc = blki * FBLK + ffl
                for tg in range(TG):
                    bank = rbank()
                    bank_wait(bank)
                    if ffl == 0 and tg == 0:
                        wait("tensor", g_wm)
                        for msr in rhs_ready:
                            wait("tensor", msr)
                    ms_mm = None
                    for cc in range(NC):
                        def f(t, cc=cc, ffl=ffl, tg=tg, bank=bank):
                            return t.matmul(
                                bank[:, 0:512],
                                Abf(WM_OFF + cc * 1024 + ffl * 128, 0, 128, 128),
                                xcur(cc)[:, tg * 512:(tg + 1) * 512],
                                start=(cc == 0), stop=(cc == NC - 1))
                        if cc == NC - 1:
                            ms_mm = op("tensor", f, inc=True)
                        else:
                            op("tensor", f)
                    last_h_mm["ms"] = ms_mm
                    wait("scalar", ms_mm)
                    if ffl == 0 and tg == 0 and prev_xnew is not None:
                        wait("scalar", prev_xnew)
                    def ev(s_, ffl=ffl, tg=tg, bank=bank):
                        return s_.activation(
                            out=hblk(ffl)[:, tg * 512:(tg + 1) * 512],
                            in_=bank[:, 0:512], func=ACT.Relu)
                    ms_e = op("scalar", ev, inc=True)
                    bank_set(bank, ms_e)
                    if tg == TG - 1:
                        relu_fin[fc] = ms_e
                if blki == 0:
                    nxt = next(tick_iter, None)
                    if nxt is not None:
                        nxt()
            if blki == 0:
                for nxt in tick_iter:
                    nxt()
                tick_iter = iter(())
            for ffl in range(FBLK):
                fc = blki * FBLK + ffl
                wait("vector", relu_fin[fc])
                if s == 0:
                    if blki == 0 and ffl == 0:
                        wait("vector", ms_mbase)
                else:
                    if blki == 0 and ffl == 0:
                        wait("vector", mask_ready_ms[s - 1])
                mcol = S[:, SO["mact"] + fc:SO["mact"] + fc + 1]
                ms_mk = op("vector", lambda v, ffl=ffl, mcol=mcol: v.tensor_scalar(
                    out=hblk(ffl), in0=hblk(ffl), scalar1=mcol, scalar2=None,
                    op0=ALU.mult), inc=True)
                relu_fin[fc] = ms_mk
            for ffl in range(FBLK):
                fc = blki * FBLK + ffl
                w = []
                if ffl == 0:
                    if last_xnew_mm["ms"] is not None:
                        w.append(last_xnew_mm["ms"])
                    if ck_read_ms["ms"] is not None:
                        w.append(ck_read_ms["ms"])
                        ck_read_ms["ms"] = None
                dma("sync", Abf(WP_OFF + ffl * 1024, 0, 128, 1024),
                    wp_bf[fc * 128:(fc + 1) * 128, :], waits=w)
            g_wp = qgate("sync")
            for cc in range(NC):
                for tg in range(TG):
                    bank = rbank()
                    bank_wait(bank)
                    if cc == 0 and tg == 0:
                        wait("tensor", g_wp)
                    ms_mm = None
                    for ffl in range(FBLK):
                        if cc == 0 and tg == 0:
                            wait("tensor", relu_fin[blki * FBLK + ffl])
                        def f(t, ffl=ffl, cc=cc, tg=tg, bank=bank):
                            return t.matmul(
                                bank[:, 0:512],
                                Abf(WP_OFF + ffl * 1024 + cc * 128, 0, 128, 128),
                                hblk(ffl)[:, tg * 512:(tg + 1) * 512],
                                start=(ffl == 0), stop=(ffl == FBLK - 1))
                        if ffl == FBLK - 1:
                            ms_mm = op("tensor", f, inc=True)
                        else:
                            op("tensor", f)
                    last_xnew_mm["ms"] = ms_mm
                    wait("vector", ms_mm)
                    if blki == 0:
                        def ac(v, cc=cc, tg=tg, bank=bank):
                            return v.tensor_copy(
                                out=xacc(cc)[:, tg * 512:(tg + 1) * 512],
                                in_=bank[:, 0:512])
                    else:
                        def ac(v, cc=cc, tg=tg, bank=bank):
                            return v.tensor_tensor(
                                out=xacc(cc)[:, tg * 512:(tg + 1) * 512],
                                in0=xacc(cc)[:, tg * 512:(tg + 1) * 512],
                                in1=bank[:, 0:512], op=ALU.add)
                    ms_a = op("vector", ac, inc=True)
                    bank_set(bank, ms_a)
        for nxt in tick_iter:
            nxt()

    def emit_pool_ck(sm):
        vdrain()
        pl_last = None
        for cc in range(NC):
            op("vector", lambda v, cc=cc: v.tensor_reduce(
                out=S[:, SO["pool"] + cc:SO["pool"] + cc + 1], in_=xacc(cc),
                axis=AX, op=ALU.add))
            vdrain()
            pl_last = op("vector", lambda v, cc=cc: v.tensor_scalar(
                out=S[:, SO["pool"] + cc:SO["pool"] + cc + 1],
                in0=S[:, SO["pool"] + cc:SO["pool"] + cc + 1],
                scalar1=1.0 / T, scalar2=None, op0=ALU.mult), inc=True)
        wait("tensor", ms_wc1)
        wait("tensor", pl_last)
        for d2c in range(2):
            bank_wait(ps6)
            ms_t = None
            for cc in range(NC):
                def mmf(t, d2c=d2c, cc=cc):
                    return t.matmul(
                        ps6[:, 0:1],
                        wc1_s[:, cc * 256 + d2c * 128:cc * 256 + (d2c + 1) * 128],
                        S[:, SO["pool"] + cc:SO["pool"] + cc + 1],
                        start=(cc == 0), stop=(cc == NC - 1))
                if cc < NC - 1:
                    op("tensor", mmf)
                else:
                    ms_t = op("tensor", mmf, inc=True)
            wait("vector", ms_t)
            ms_y1 = op("vector", lambda v, d2c=d2c: v.tensor_scalar(
                out=S[:, SO["y1"] + d2c:SO["y1"] + d2c + 1], in0=ps6[:, 0:1],
                scalar1=0.0, scalar2=None, op0=ALU.max), inc=True)
            bank_set(ps6, ms_y1)
        vdrain()
        y1b = S.ap().bitcast(BF16)[0:128, 2 * SO["y1b"]:2 * SO["y1b"] + 2]
        ms_y1b = op("vector", lambda v, y1b=y1b: v.tensor_copy(
            out=y1b, in_=S[:, SO["y1"]:SO["y1"] + 2]), inc=True)
        bank_wait(ps4)
        ck_done = None
        prev_g = last_h_mm["ms"]
        for g in range(3):
            dma("sync",
                Abf(WM_OFF, 0, 128, 8192).rearrange("p (d q) -> p d q", d=2),
                col_block(wc2_bf, g * 4096, 4096, 2, FF),
                waits=[prev_g])
            ms_w = qgate("sync")
            wait("tensor", ms_w)
            if g == 0:
                wait("tensor", ms_y1b)
            for fi in range(32):
                fc = g * 32 + fi
                for d2c in range(2):
                    def mmf(t, fi=fi, d2c=d2c, fc=fc):
                        return t.matmul(
                            ps4[:, fc:fc + 1],
                            Abf(WM_OFF + d2c * 4096 + fi * 128, 0, 128, 128),
                            S.ap().bitcast(BF16)[0:128,
                                                 2 * SO["y1b"] + d2c:
                                                 2 * SO["y1b"] + d2c + 1],
                            start=(d2c == 0), stop=(d2c == 1))
                    last = (fi == 31 and d2c == 1)
                    if last:
                        prev_g = op("tensor", mmf, inc=True)
                        if fc == FC - 1:
                            ck_done = prev_g
                    else:
                        op("tensor", mmf)
        wait("vector", ck_done)
        ms_ck = op("vector", lambda v: v.tensor_copy(
            out=S[:, SO["ck"]:SO["ck"] + 96], in_=ps4[:, 0:96]), inc=True)
        bank_set(ps4, ms_ck)
        ck_read_ms["ms"] = ck_done
        last_h_mm["ms"] = ck_done
        if "d_sc" in dbg:
            dma("sync", dbg["d_sc"][sm], S[:, SO["ck"]:SO["ck"] + 96],
                waits=[ms_ck])
        return ms_ck

    rhs0 = [m for m in x1_cast_ms]
    emit_step(0, rhs0)
    ms_xa0 = ("v", cnt["v"])
    ms_ck0 = emit_pool_ck(0)
    rhs1 = []
    for cc in range(NC):
        dma("sync", xn1_spill[cc * 128:(cc + 1) * 128, :], xacc(cc),
            waits=[ms_xa0])
        wait("scalar", ms_xa0)
        ms_c = op("scalar", lambda s_, cc=cc: s_.activation(
            out=xcur(cc), in_=xacc(cc), func=ACT.Identity), inc=True)
        rhs1.append(ms_c)
        if "d_xn" in dbg:
            dma("sync", dbg["d_xn"][0, cc * 128:(cc + 1) * 128, :], xacc(cc),
                waits=[ms_xa0])
    g_sp1 = qgate("sync")
    bis_prepass(0, ms_ck0)
    wait("vector", g_sp1)
    for m in rhs1:
        wait("vector", m)
    emit_step(1, rhs1)
    bis_nov_finish(0)
    ms_xa1 = ("v", cnt["v"])
    ms_ck1 = emit_pool_ck(1)
    rhs2 = []
    for cc in range(NC):
        xre = Af32(ROTF, 0, 128, 2048)
        dma("sync", xre, xn1_spill[cc * 128:(cc + 1) * 128, :],
            waits=[m for m in readers.pop(("rotf",), [])])
        ms_re = qgate("sync")
        wait("vector", ms_re)
        if cc == 0:
            wait("vector", nov_done_ms[0])
        d1 = S[:, SO["drep"]:SO["drep"] + 1]
        vdrain()
        op("vector", lambda v, xre=xre, cc=cc: v.tensor_tensor(
            out=xre, in0=xre, in1=xacc(cc), op=ALU.subtract))
        vdrain()
        op("vector", lambda v, xre=xre, d1=d1: v.tensor_scalar(
            out=xre, in0=xre, scalar1=d1, scalar2=None, op0=ALU.mult))
        vdrain()
        ms_bl = op("vector", lambda v, xre=xre, cc=cc: v.tensor_tensor(
            out=xacc(cc), in0=xacc(cc), in1=xre, op=ALU.add), inc=True)
        note_read(("rotf",), ms_bl)
        dma("sync", xo2_spill[cc * 128:(cc + 1) * 128, :], xacc(cc),
            waits=[ms_bl])
        wait("scalar", ms_bl)
        ms_c = op("scalar", lambda s_, cc=cc: s_.activation(
            out=xcur(cc), in_=xacc(cc), func=ACT.Identity), inc=True)
        rhs2.append(ms_c)
        if "d_xn" in dbg:
            dma("sync", dbg["d_xn"][1, cc * 128:(cc + 1) * 128, :], xacc(cc),
                waits=[ms_bl])
    g_sp2 = qgate("sync")
    bis_prepass(1, ms_ck1)
    wait("vector", g_sp2)
    for m in rhs2:
        wait("vector", m)
    emit_step(2, rhs2)
    bis_nov_finish(1)
    fin_ms = [None] * NC
    for cc in range(NC):
        xre = Af32(ROTF, 0, 128, 2048)
        ms_xa = ("v", cnt["v"])
        dma("sync", xre, xo2_spill[cc * 128:(cc + 1) * 128, :],
            waits=[m for m in readers.pop(("rotf",), [])])
        ms_re = qgate("sync")
        wait("vector", ms_re)
        if cc == 0:
            wait("vector", nov_done_ms[1])
        d2v = S[:, SO["drep"] + 1:SO["drep"] + 2]
        vdrain()
        op("vector", lambda v, xre=xre, cc=cc: v.tensor_tensor(
            out=xre, in0=xre, in1=xacc(cc), op=ALU.subtract))
        vdrain()
        op("vector", lambda v, xre=xre, d2v=d2v: v.tensor_scalar(
            out=xre, in0=xre, scalar1=d2v, scalar2=None, op0=ALU.mult))
        vdrain()
        ms_f = op("vector", lambda v, xre=xre, cc=cc: v.tensor_tensor(
            out=xacc(cc), in0=xacc(cc), in1=xre, op=ALU.add), inc=True)
        note_read(("rotf",), ms_f)
        fin_ms[cc] = ms_f
        if "d_xn" in dbg:
            dma("sync", dbg["d_xn"][2, cc * 128:(cc + 1) * 128, :], xacc(cc),
                waits=[ms_f])
    if "d_msk" in dbg:
        dma("sync", dbg["d_msk"][0], S[:, SO["mact"]:SO["mact"] + 96],
            waits=[mask_ready_ms[1]])
        dma("sync", dbg["d_msk"][1], S[:, SO["hist"]:SO["hist"] + 96])
        dma("sync", dbg["d_msk"][2], S[:, SO["mnew"]:SO["mnew"] + 96])
        dma("sync", dbg["d_msk"][3], S[:, SO["drep"]:SO["drep"] + 2].broadcast_to(
            [128, 96]) if False else S[:, SO["cmb"]:SO["cmb"] + 96])

    if stage == "ffn":
        outs = [dma("sync", out_d[:, :], x1_spill[:, :], waits=[fin_ms[-1]])]
        return finish(outs)

    lastT_ffn = ("t", cnt["t"])
    g_ffn_dg = qgate("gpsimd")

    def fin_src(m, ybuf):
        xre = Af32(XREROT[m % 2], 0, 128, 2048)
        dma("sync", xre, x1_spill[m * 128:(m + 1) * 128, :],
            waits=[m2 for m2 in readers.pop(("xrerot", m % 2), [])]
            + ([lastT_ffn, g_ffn_dg] if m < 2 else []))
        ms_x = qgate("sync")
        wait("vector", ms_x)
        if m == 0:
            wait("vector", lastT_ffn)
        wait("vector", fin_ms[m])
        vdrain()
        ms_y = op("vector", lambda v, xre=xre, m=m, ybuf=ybuf: v.tensor_tensor(
            out=ybuf, in0=xacc(m), in1=xre, op=ALU.add), inc=True)
        note_read(("xrerot", m % 2), ms_y)
        return ms_y

    out_ms, _ = emit_ln(fin_src, z_spill, out_d, None, [],
                        ybuf_fn=lambda m: xacc(m))
    return finish(out_ms)


def _prep_weights(ins):
    rel = np.asarray(ins["L0_attn_rel_emb"], np.float32)
    pe = np.arange(4096)
    src = np.clip(3070 - pe, 0, 2046)
    rextT = np.ascontiguousarray(rel[src].T)
    return {
        "wq": np.ascontiguousarray(ins["L0_attn_Wq"], dtype=np.float32),
        "wk": np.ascontiguousarray(ins["L0_attn_Wk"], dtype=np.float32),
        "wv": np.ascontiguousarray(ins["L0_attn_Wv"], dtype=np.float32),
        "wo": np.ascontiguousarray(ins["L0_attn_Wo"], dtype=np.float32),
        "bq": np.ascontiguousarray(
            np.asarray(ins["L0_attn_bq"], np.float32).reshape(8, 128).T),
        "bk": np.ascontiguousarray(
            np.asarray(ins["L0_attn_bk"], np.float32).reshape(8, 128).T),
        "bv": np.ascontiguousarray(
            np.asarray(ins["L0_attn_bv"], np.float32).reshape(8, 128).T),
        "bo": np.ascontiguousarray(
            np.asarray(ins["L0_attn_bo"], np.float32).reshape(8, 128).T),
        "rextT": rextT,
        "wmT": np.ascontiguousarray(np.asarray(ins["L0_ffn_Wm"], np.float32).T),
        "wpT": np.ascontiguousarray(np.asarray(ins["L0_ffn_Wp"], np.float32).T),
        "wc1T": np.ascontiguousarray(np.asarray(ins["L0_ffn_Wc1"], np.float32).T),
        "wc2T": np.ascontiguousarray(np.asarray(ins["L0_ffn_Wc2"], np.float32).T),
        "onesf": np.ones((128, 128), np.float32),
        "identb": np.eye(128, dtype=ml_dtypes.bfloat16),
    }


def kernel(**inputs):
    x = np.asarray(inputs["L0_attn_x"], np.float32)
    w = _prep_weights(inputs)
    nc, _ = build_nc("full")
    in_maps = []
    for b in range(8):
        m = dict(w)
        m["xT"] = np.ascontiguousarray(x[b].T)
        in_maps.append(m)
    res = run_bass_kernel_spmd(nc, in_maps, core_ids=list(range(8)))
    out = np.stack([np.ascontiguousarray(res.results[b]["outT"].T)
                    for b in range(8)])
    return out.astype(np.float32)
